# version 22
# baseline (speedup 1.0000x reference)
import numpy as np
import ml_dtypes
import concourse.bass as bass
import concourse.mybir as mybir
from concourse.bass_utils import run_bass_kernel_spmd
from contextlib import ExitStack

F32 = mybir.dt.float32
BF16 = mybir.dt.bfloat16
AF = mybir.ActivationFunctionType
ALU = mybir.AluOpType
AX = mybir.AxisListType

D = 2048
DC = 512
DQC = 1024
H = 16
DH = 128
FF = 1024
E = 8
NCORE = 8
NEG = -1.0e9
EPS = 1e-5
DcK = D // 128

STREAM_OF = {"sync": "sync", "gdma": "gpsimd", "pool": "gpsimd",
             "pe": "tensor", "dve": "vector", "act": "scalar"}
SEM_KEYS = ("pool", "pe", "dve", "act")
DMA_POOL = {"sync": 12, "gdma": 4}


class Buf:
    __slots__ = ("w", "r")

    def __init__(self):
        self.w = []
        self.r = []


class Prog:
    def __init__(self, nc, stack):
        self.nc = nc
        self.stack = stack
        self.eng = {"sync": nc.sync, "gpsimd": nc.gpsimd, "tensor": nc.tensor,
                    "vector": nc.vector, "scalar": nc.scalar}
        self.sems = {}
        self.cnt = {}
        self.seen = {s: {} for s in self.eng}
        self.bufs = []
        self.pnum = 0
        self._mk_sems()

    def _mk_sems(self):
        self.pnum += 1
        self.sems = {}
        self.cnt = {}
        for key in SEM_KEYS:
            self.sems[key] = self.stack.enter_context(
                self.nc.semaphore(f"p{self.pnum}_{key}"))
            self.cnt[key] = 0
        for q, n in DMA_POOL.items():
            for j in range(n):
                key = f"{q}{j}"
                self.sems[key] = self.stack.enter_context(
                    self.nc.semaphore(f"p{self.pnum}_{key}"))
                self.cnt[key] = 0
        self.rr = {q: 0 for q in DMA_POOL}
        for s in self.seen:
            self.seen[s] = {}

    def buf(self):
        b = Buf()
        self.bufs.append(b)
        return b

    def barrier(self):
        final = {k: v for k, v in self.cnt.items() if v > 0}
        for s, e in self.eng.items():
            for k, v in final.items():
                if self.seen[s].get(k, 0) < v:
                    e.wait_ge(self.sems[k], v)
                    self.seen[s][k] = v
        for b in self.bufs:
            b.w = []
            b.r = []

    def emit(self, key, build, reads=(), writes=(), deps=()):
        stream = STREAM_OF[key[:4] if key[:4] in ("sync", "gdma") else key]
        e = self.eng[stream]
        toks = []
        for b in reads:
            toks.extend(b.w)
        for b in writes:
            toks.extend(b.w)
            toks.extend(b.r)
        toks.extend(deps)
        for (k, v) in toks:
            if stream == "tensor" and k == "pe":
                continue
            if self.seen[stream].get(k, 0) < v:
                e.wait_ge(self.sems[k], v)
                self.seen[stream][k] = v
        inst = build(e)
        inc = 16 if key[:4] in ("sync", "gdma") else 1
        inst.then_inc(self.sems[key], inc)
        self.cnt[key] += inc
        tok = (key, self.cnt[key])
        for b in reads:
            b.r.append(tok)
        for b in writes:
            b.w = [tok]
            b.r = []
        return tok

    def dma(self, key, dst, src, reads=(), writes=(), deps=()):
        q = key
        j = self.rr[q] % DMA_POOL[q]
        self.rr[q] += 1
        slot = f"{q}{j}"
        deps = list(deps)
        if self.cnt[slot] > 0:
            deps.append((slot, self.cnt[slot]))
        return self.emit(slot, lambda e: e.dma_start(dst, src),
                         reads=reads, writes=writes, deps=deps)

    def seal(self, q, buf):
        buf.w = [(f"{q}{j}", self.cnt[f"{q}{j}"]) for j in range(DMA_POOL[q])
                 if self.cnt[f"{q}{j}"] > 0]
        buf.r = []


def mm_group(p, psum_ap, pairs, reads, writes):
    def build(t):
        last = None
        n = len(pairs)
        for i, (l, r) in enumerate(pairs):
            last = t.matmul(psum_ap, l, r, start=(i == 0), stop=(i == n - 1))
        return last
    return p.emit("pe", build, reads=reads, writes=writes)


def build_program(S=4096, debug=()):
    assert S % 512 == 0
    NT = S // 512
    CF = min(1024, S)
    NFC = S // CF
    NTC = CF // 512
    NQS = S // 512

    nc = bass.Bass("TRN2", target_bir_lowering=False)
    dbg = lambda name: "ExternalOutput" if name in debug else "Internal"

    x_pk = nc.dram_tensor("x_pk", [S // 512, 128, DcK * 512], F32,
                          kind="ExternalInput")
    wdkv_t = nc.dram_tensor("wdkv_t", [D, DC], F32, kind="ExternalInput")
    wuk_t = nc.dram_tensor("wuk_t", [DC, D], F32, kind="ExternalInput")
    wuv_t = nc.dram_tensor("wuv_t", [DC, D], F32, kind="ExternalInput")
    wdq_t = nc.dram_tensor("wdq_t", [D, DQC], F32, kind="ExternalInput")
    wuq_t = nc.dram_tensor("wuq_t", [DQC, D], F32, kind="ExternalInput")
    wkr_t = nc.dram_tensor("wkr_t", [D, D], F32, kind="ExternalInput")
    wo_t = nc.dram_tensor("wo_t", [D, D], F32, kind="ExternalInput")
    wg_t = nc.dram_tensor("wg_t", [D, E], F32, kind="ExternalInput")
    w1_t = nc.dram_tensor("w1_t", [E, D, FF], F32, kind="ExternalInput")
    w3_t = nc.dram_tensor("w3_t", [E, D, FF], F32, kind="ExternalInput")
    w2_t = nc.dram_tensor("w2_t", [E, FF, D], F32, kind="ExternalInput")
    b_uv_d = nc.dram_tensor("b_uv", [D], F32, kind="ExternalInput")
    bias_pk_d = nc.dram_tensor("bias_pk", [128, 76], F32, kind="ExternalInput")
    fbias_d = nc.dram_tensor("fbias", [E], F32, kind="ExternalInput")
    ident_bf_d = nc.dram_tensor("ident_bf", [128, 128], BF16, kind="ExternalInput")
    ident_f32_d = nc.dram_tensor("ident_f32", [128, 128], F32, kind="ExternalInput")
    pswap_d = nc.dram_tensor("pswap", [128, 128], BF16, kind="ExternalInput")
    ones_col_d = nc.dram_tensor("ones_col", [128, 1], BF16, kind="ExternalInput")
    ropec_d = nc.dram_tensor("ropec", [128, S], BF16, kind="ExternalInput")
    ropes_d = nc.dram_tensor("ropes", [128, S], BF16, kind="ExternalInput")
    masks_d = nc.dram_tensor("masks", [128, 4 * 512], F32, kind="ExternalInput")
    iota8_d = nc.dram_tensor("iota8", [128, 32], F32, kind="ExternalInput")
    iota8m_d = nc.dram_tensor("iota8m", [128, 32], F32, kind="ExternalInput")

    out_fm = nc.dram_tensor("out_fm", [D, S], F32, kind="ExternalOutput")
    xn_pk = nc.dram_tensor("xn_pk", [S // 512, 128, DcK * 512], BF16)
    kc_d = nc.dram_tensor("kc_d", [D, S], BF16, kind=dbg("kc_d"))
    kr_d = nc.dram_tensor("kr_d", [D, S], BF16, kind=dbg("kr_d"))
    qc_d = nc.dram_tensor("qc_d", [D, S], BF16, kind=dbg("qc_d"))
    qr_d = nc.dram_tensor("qr_d", [D, S], BF16, kind=dbg("qr_d"))
    v_d = nc.dram_tensor("v_d", [S, D], BF16, kind=dbg("v_d"))
    o_d = nc.dram_tensor("o_d", [D, S], BF16, kind=dbg("o_d"))
    part_d = nc.dram_tensor("part_d", [D, S], F32, kind=dbg("part_d"))
    ew_d = nc.dram_tensor("ew_d", [E, S], BF16, kind=dbg("ew_d"))
    rows_d = nc.dram_tensor("rows_d", [NT, 2, 512], BF16)
    w1b_d = nc.dram_tensor("w1b_d", [E, D, FF], BF16)
    w3b_d = nc.dram_tensor("w3b_d", [E, D, FF], BF16)
    w2b_d = nc.dram_tensor("w2b_d", [E, FF, D], BF16)
    wkrb_d = nc.dram_tensor("wkrb_d", [D, D], BF16)
    wdkvb_d = nc.dram_tensor("wdkvb_d", [D, DC], BF16)
    wukb_d = nc.dram_tensor("wukb_d", [DC, D], BF16)
    wuvb_d = nc.dram_tensor("wuvb_d", [DC, D], BF16)
    wdqb_d = nc.dram_tensor("wdqb_d", [D, DQC], BF16)
    wuqb_d = nc.dram_tensor("wuqb_d", [DQC, D], BF16)
    wob_d = nc.dram_tensor("wob_d", [D, D], BF16)

    stack = ExitStack()
    p = Prog(nc, stack)
    sb = lambda name, shape, dt: stack.enter_context(nc.sbuf_tensor(name, shape, dt))

    id_bf = sb("id_bf", [128, 128], BF16)
    id_f = sb("id_f", [128, 128], F32)
    psw = sb("psw", [128, 128], BF16)
    ones_c = sb("ones_c", [128, 1], BF16)
    ewT_s = sb("ewT_s", [8, S], BF16)
    bias_s = sb("bias_s", [128, 76], F32)
    b_id = p.buf(); b_idf = p.buf(); b_psw = p.buf(); b_ones = p.buf()
    b_ewT = p.buf(); b_bias = p.buf()

    p.dma("gdma", id_bf[:, :], ident_bf_d[:, :], writes=[b_id])
    p.dma("sync", id_f[:, :], ident_f32_d[:, :], writes=[b_idf])
    p.dma("gdma", psw[:, :], pswap_d[:, :], writes=[b_psw])
    p.dma("gdma", ones_c[:, :], ones_col_d[:, :], writes=[b_ones])
    p.dma("sync", bias_s[:, 0:76], bias_pk_d[:, :])
    p.seal("sync", b_bias)

    bg_sem = stack.enter_context(nc.semaphore("bg_conv"))
    bg_jobs = []
    def add_cast(dst, src, rows):
        step = max(rows // max((rows * dst.shape[-1] * 4) // (4 << 20), 1), 1)
        r = 0
        while r < rows:
            r2 = min(r + step, rows)
            bg_jobs.append((dst[r:r2, :], src[r:r2, :], bg_sem))
            r = r2

    add_cast(wkrb_d, wkr_t, D)
    add_cast(wdkvb_d, wdkv_t, D)
    add_cast(wukb_d, wuk_t, DC)
    add_cast(wuvb_d, wuv_t, DC)
    add_cast(wdqb_d, wdq_t, D)
    add_cast(wuqb_d, wuq_t, DQC)
    add_cast(wob_d, wo_t, D)
    N_BG_ATTN = len(bg_jobs)
    for e_ in range(E):
        for (src, dst) in ((w1_t, w1b_d), (w3_t, w3b_d)):
            bg_jobs.append((dst[e_, 0:D // 2, :], src[e_, 0:D // 2, :], bg_sem))
            bg_jobs.append((dst[e_, D // 2:D, :], src[e_, D // 2:D, :], bg_sem))
        bg_jobs.append((w2b_d[e_, 0:FF // 2, :], w2_t[e_, 0:FF // 2, :], bg_sem))
        bg_jobs.append((w2b_d[e_, FF // 2:FF, :], w2_t[e_, FF // 2:FF, :], bg_sem))
    bg_state = {"i": 0}

    def pump_bg(n=2):
        while n > 0 and bg_state["i"] < len(bg_jobs):
            dst, src, sem = bg_jobs[bg_state["i"]]
            nc.gpsimd.dma_start(dst, src).then_inc(sem, 16)
            bg_state["i"] += 1
            n -= 1

    def stream_weight_bf(w_src_bf, w_sbuf, ncols, width, nK, bufs):
        src_r = w_src_bf.rearrange("(c q) n -> q c n", q=128)
        dst_r = w_sbuf.rearrange("q (c n) -> q c n", n=width)
        for c in range(nK):
            p.dma("sync", dst_r[:, c, :], src_r[:, c, :])
        for b in bufs:
            p.seal("sync", b)

    N_BG_WKR = 4
    pump_bg(N_BG_WKR)
    p.barrier()

    ph = ExitStack()
    sbp = lambda name, shape, dt: ph.enter_context(nc.sbuf_tensor(name, shape, dt))
    psp = lambda name, shape, dt: ph.enter_context(nc.psum_tensor(name, shape, dt))
    xc = [sbp(f"xc{i}", [128, DcK * 512], F32) for i in range(2)]
    b_xc = [p.buf() for _ in range(2)]
    xb = [sbp(f"xb{i}", [128, DcK * 512], BF16) for i in range(2)]
    b_xb = [p.buf() for _ in range(2)]
    xsq = [sbp(f"xsq{i}", [128, DcK * 512], BF16) for i in range(2)]
    b_xsq = [p.buf() for _ in range(2)]
    xnb = [sbp(f"xnb{i}", [128, DcK * 512], BF16) for i in range(2)]
    b_xnb = [p.buf() for _ in range(2)]
    srow = sbp("srow", [1, 512], F32); b_srow = p.buf()
    sqrow = sbp("sqrow", [1, 512], F32); b_sqrow = p.buf()
    varrow = sbp("varrow", [1, 512], F32); b_varrow = p.buf()
    rstd_r = sbp("rstd_r", [1, 512], BF16); b_rstd = p.buf()
    negmr_r = sbp("negmr_r", [1, 512], BF16); b_negmr = p.buf()
    rstd_b = [sbp(f"rstd_b{i}", [128, 512], BF16) for i in range(2)]
    b_rstdb = [p.buf() for _ in range(2)]
    negmr_b = [sbp(f"negmr_b{i}", [128, 512], BF16) for i in range(2)]
    b_negmrb = [p.buf() for _ in range(2)]
    t1 = sbp("t1", [128, 512], BF16); b_t1 = p.buf()
    wg_s = sbp("wg_s", [128, DcK * 8], BF16); b_wg = p.buf()
    fbias_s = sbp("fbias_s", [128, 32], F32); b_fbias = p.buf()
    iota_s = sbp("iota_s", [128, 32], F32); b_iota = p.buf()
    iotam_s = sbp("iotam_s", [128, 32], F32); b_iotam = p.buf()
    rt = sbp("rt", [128, 32], F32); b_rt = p.buf()
    rw = sbp("rw", [128, 32], F32); b_rw = p.buf()
    sc_ = sbp("sc_", [128, 32], F32); b_sc = p.buf()
    ta = sbp("ta", [128, 32], F32); b_ta = p.buf()
    tb = sbp("tb", [128, 32], F32); b_tb = p.buf()
    eqa = sbp("eqa", [128, 32], F32); b_eqa = p.buf()
    eqb = sbp("eqb", [128, 32], F32); b_eqb = p.buf()
    m1c = sbp("m1c", [128, 4], F32); b_m1 = p.buf()
    i1c = sbp("i1c", [128, 4], F32); b_i1 = p.buf()
    m2c = sbp("m2c", [128, 4], F32); b_m2 = p.buf()
    i2c = sbp("i2c", [128, 4], F32); b_i2 = p.buf()
    w1c = sbp("w1c", [128, 4], F32); b_w1 = p.buf()
    w2c = sbp("w2c", [128, 4], F32); b_w2 = p.buf()
    ewc = sbp("ewc", [128, 32], F32); b_ew = p.buf()
    ewbig = sbp("ewbig", [128, 128], F32); b_ewbig = p.buf()
    ps_s1 = [psp(f"ps_s1{i}", [128, 512], F32) for i in range(2)]
    b_pss1 = [p.buf() for _ in range(2)]
    ps_s2 = [psp(f"ps_s2{i}", [128, 512], F32) for i in range(2)]
    b_pss2 = [p.buf() for _ in range(2)]
    ps_rt = psp("ps_rt", [128, 512], F32); b_psrt = p.buf()
    ps_ew = psp("ps_ew", [128, 512], F32); b_psew = p.buf()
    b_rowsd = [p.buf() for _ in range(NT)]

    p.dma("gdma", wg_s.rearrange("q (c n) -> q c n", n=8),
          wg_t.rearrange("(c q) n -> q c n", q=128), writes=[b_wg])
    p.dma("sync", fbias_s[:, :].rearrange("q (t e) -> q t e", e=8),
          fbias_d.rearrange("(one e) -> one e", one=1).unsqueeze(1)
          .broadcast_to([128, 4, 8]), writes=[b_fbias])
    p.dma("sync", iota_s[:, :], iota8_d[:, :], writes=[b_iota])
    p.dma("sync", iotam_s[:, :], iota8m_d[:, :], writes=[b_iotam])
    p.emit("dve", lambda v: v.memset(ewbig[:, :], 0.0), writes=[b_ewbig])

    for tc in range(NT):
        sl = tc % 2
        tok = slice(tc * 512, (tc + 1) * 512)
        for qu in range(4):
            csl = slice(qu * 2048, (qu + 1) * 2048)
            p.dma("sync", xc[sl][:, csl], x_pk[tc, :, csl], writes=[b_xc[sl]])
        for c in range(DcK):
            cs = slice(c * 512, (c + 1) * 512)
            p.emit("act", lambda s, cs=cs: s.copy(xb[sl][:, cs], xc[sl][:, cs]),
                   reads=[b_xc[sl]], writes=[b_xb[sl]])
            p.emit("dve", lambda v, cs=cs: v.tensor_mul(xsq[sl][:, cs], xb[sl][:, cs],
                                                        xb[sl][:, cs]),
                   reads=[b_xb[sl]], writes=[b_xsq[sl]])
        mm_group(p, ps_s1[sl][0:1, :],
                 [(ones_c[:, :], xb[sl][:, c * 512:(c + 1) * 512]) for c in range(DcK)],
                 reads=[b_ones, b_xb[sl]], writes=[b_pss1[sl]])
        mm_group(p, ps_s2[sl][0:1, :],
                 [(ones_c[:, :], xsq[sl][:, c * 512:(c + 1) * 512]) for c in range(DcK)],
                 reads=[b_ones, b_xsq[sl]], writes=[b_pss2[sl]])
        p.emit("dve", lambda v: v.tensor_scalar(srow[0:1, :], ps_s1[sl][0:1, :],
                                                1.0 / D, None, ALU.mult),
               reads=[b_pss1[sl]], writes=[b_srow])
        p.emit("dve", lambda v: v.tensor_scalar(sqrow[0:1, :], ps_s2[sl][0:1, :],
                                                1.0 / D, None, ALU.mult),
               reads=[b_pss2[sl]], writes=[b_sqrow])
        p.emit("dve", lambda v: v.tensor_mul(varrow[0:1, :], srow[0:1, :], srow[0:1, :]),
               reads=[b_srow], writes=[b_varrow])
        p.emit("dve", lambda v: v.tensor_sub(varrow[0:1, :], sqrow[0:1, :], varrow[0:1, :]),
               reads=[b_sqrow], writes=[b_varrow])
        p.emit("dve", lambda v: v.tensor_scalar(varrow[0:1, :], varrow[0:1, :], EPS,
                                                None, ALU.add),
               writes=[b_varrow])
        p.emit("act", lambda s: s.activation(varrow[0:1, :], varrow[0:1, :], AF.Ln),
               writes=[b_varrow])
        p.emit("act", lambda s: s.activation(rstd_r[0:1, :], varrow[0:1, :], AF.Exp,
                                             bias=0.0, scale=-0.5),
               reads=[b_varrow], writes=[b_rstd])
        p.emit("dve", lambda v: v.scalar_tensor_tensor(negmr_r[0:1, :], srow[0:1, :],
                                                       -1.0, rstd_r[0:1, :],
                                                       ALU.mult, ALU.mult),
               reads=[b_srow, b_rstd], writes=[b_negmr])
        p.dma("sync", rows_d[tc, 0:1, :], rstd_r[0:1, :], reads=[b_rstd],
              writes=[b_rowsd[tc]])
        p.dma("sync", rows_d[tc, 1:2, :], negmr_r[0:1, :], reads=[b_negmr],
              writes=[b_rowsd[tc]])
        p.dma("sync", rstd_b[sl][:, :], rows_d[tc, 0:1, :].broadcast_to([128, 512]),
              reads=[b_rowsd[tc]], writes=[b_rstdb[sl]])
        p.dma("sync", negmr_b[sl][:, :], rows_d[tc, 1:2, :].broadcast_to([128, 512]),
              reads=[b_rowsd[tc]], writes=[b_negmrb[sl]])
        for c in range(DcK):
            cs = slice(c * 512, (c + 1) * 512)
            p.emit("dve", lambda v, cs=cs: v.tensor_mul(t1[:, :], xb[sl][:, cs],
                                                        rstd_b[sl][:, :]),
                   reads=[b_xb[sl], b_rstdb[sl]], writes=[b_t1])
            p.emit("dve", lambda v, cs=cs: v.tensor_add(xnb[sl][:, cs], t1[:, :],
                                                        negmr_b[sl][:, :]),
                   reads=[b_t1, b_negmrb[sl]], writes=[b_xnb[sl]])
        for qu in range(2):
            csl = slice(qu * 4096, (qu + 1) * 4096)
            p.dma("sync", xn_pk[tc, :, csl], xnb[sl][:, csl], reads=[b_xnb[sl]])

        for tt in range(4):
            mm_group(p, ps_rt[:, tt * 8:(tt + 1) * 8],
                     [(xnb[sl][:, c * 512 + tt * 128: c * 512 + (tt + 1) * 128],
                       wg_s[:, c * 8:(c + 1) * 8]) for c in range(DcK)],
                     reads=[b_xnb[sl], b_wg], writes=[b_psrt])
        p.emit("dve", lambda v: v.tensor_copy(rt[:, :], ps_rt[:, 0:32]),
               reads=[b_psrt], writes=[b_rt])
        p.emit("act", lambda s: s.activation(rw[:, :], rt[:, :], AF.Exp,
                                             bias=0.0, scale=-1.0),
               reads=[b_rt], writes=[b_rw])
        p.emit("dve", lambda v: v.tensor_scalar(rw[:, :], rw[:, :], 1.0, None, ALU.add),
               writes=[b_rw])
        p.emit("dve", lambda v: v.reciprocal(rw[:, :], rw[:, :]), writes=[b_rw])
        p.emit("dve", lambda v: v.tensor_add(sc_[:, :], rw[:, :], fbias_s[:, :]),
               reads=[b_rw, b_fbias], writes=[b_sc])
        sg = lambda t_: t_[:, :].rearrange("q (t e) -> q t e", e=8)
        bc = lambda t_: t_[:, :].unsqueeze(2).broadcast_to([128, 4, 8])
        p.emit("dve", lambda v: v.reduce_max(m1c[:, :], sg(sc_), axis=AX.X),
               reads=[b_sc], writes=[b_m1])
        p.emit("dve", lambda v: v.tensor_tensor(ta[:, :], sc_[:, :], bc(m1c), ALU.is_ge),
               reads=[b_sc, b_m1], writes=[b_ta])
        p.emit("dve", lambda v: v.tensor_mul(tb[:, :], ta[:, :], iotam_s[:, :]),
               reads=[b_ta, b_iotam], writes=[b_tb])
        p.emit("dve", lambda v: v.tensor_scalar(tb[:, :], tb[:, :], 99.0, None, ALU.add),
               writes=[b_tb])
        p.emit("dve", lambda v: v.tensor_reduce(i1c[:, :], sg(tb), axis=AX.X, op=ALU.min),
               reads=[b_tb], writes=[b_i1])
        p.emit("dve", lambda v: v.tensor_tensor(eqa[:, :], iota_s[:, :], bc(i1c),
                                                ALU.is_equal),
               reads=[b_iota, b_i1], writes=[b_eqa])
        p.emit("dve", lambda v: v.scalar_tensor_tensor(ta[:, :], eqa[:, :], NEG,
                                                       sc_[:, :], ALU.mult, ALU.add),
               reads=[b_eqa, b_sc], writes=[b_ta])
        p.emit("dve", lambda v: v.reduce_max(m2c[:, :], sg(ta), axis=AX.X),
               reads=[b_ta], writes=[b_m2])
        p.emit("dve", lambda v: v.tensor_tensor(tb[:, :], ta[:, :], bc(m2c), ALU.is_ge),
               reads=[b_ta, b_m2], writes=[b_tb])
        p.emit("dve", lambda v: v.tensor_mul(tb[:, :], tb[:, :], iotam_s[:, :]),
               reads=[b_iotam], writes=[b_tb])
        p.emit("dve", lambda v: v.tensor_scalar(tb[:, :], tb[:, :], 99.0, None, ALU.add),
               writes=[b_tb])
        p.emit("dve", lambda v: v.tensor_reduce(i2c[:, :], sg(tb), axis=AX.X, op=ALU.min),
               reads=[b_tb], writes=[b_i2])
        p.emit("dve", lambda v: v.tensor_tensor(eqb[:, :], iota_s[:, :], bc(i2c),
                                                ALU.is_equal),
               reads=[b_iota, b_i2], writes=[b_eqb])
        p.emit("dve", lambda v: v.tensor_mul(ta[:, :], rw[:, :], eqa[:, :]),
               reads=[b_rw, b_eqa], writes=[b_ta])
        p.emit("dve", lambda v: v.reduce_sum(w1c[:, :], sg(ta), axis=AX.X),
               reads=[b_ta], writes=[b_w1])
        p.emit("dve", lambda v: v.tensor_mul(tb[:, :], rw[:, :], eqb[:, :]),
               reads=[b_rw, b_eqb], writes=[b_tb])
        p.emit("dve", lambda v: v.reduce_sum(w2c[:, :], sg(tb), axis=AX.X),
               reads=[b_tb], writes=[b_w2])
        p.emit("dve", lambda v: v.tensor_add(m1c[:, :], w1c[:, :], w2c[:, :]),
               reads=[b_w1, b_w2], writes=[b_m1])
        p.emit("dve", lambda v: v.reciprocal(m1c[:, :], m1c[:, :]), writes=[b_m1])
        p.emit("dve", lambda v: v.tensor_tensor(ta[:, :], eqa[:, :], bc(w1c), ALU.mult),
               reads=[b_eqa, b_w1], writes=[b_ta])
        p.emit("dve", lambda v: v.tensor_tensor(ta[:, :], ta[:, :], bc(m1c), ALU.mult),
               reads=[b_m1], writes=[b_ta])
        p.emit("dve", lambda v: v.tensor_tensor(tb[:, :], eqb[:, :], bc(w2c), ALU.mult),
               reads=[b_eqb, b_w2], writes=[b_tb])
        p.emit("dve", lambda v: v.tensor_tensor(tb[:, :], tb[:, :], bc(m1c), ALU.mult),
               reads=[b_m1], writes=[b_tb])
        p.emit("dve", lambda v: v.tensor_add(ewc[:, :], ta[:, :], tb[:, :]),
               reads=[b_ta, b_tb], writes=[b_ew])
        for tt in range(4):
            p.emit("dve", lambda v, tt=tt: v.tensor_copy(ewbig[:, 0:8],
                                                         ewc[:, tt * 8:(tt + 1) * 8]),
                   reads=[b_ew], writes=[b_ewbig])
            p.emit("pe", lambda t: t.transpose(ps_ew[:, 0:128], ewbig[:, :], id_f[:, :]),
                   reads=[b_ewbig, b_idf], writes=[b_psew])
            p.emit("dve", lambda v, tt=tt: v.tensor_copy(
                ewT_s[0:8, tc * 512 + tt * 128: tc * 512 + (tt + 1) * 128],
                ps_ew[0:8, 0:128]),
                   reads=[b_psew], writes=[b_ewT])
        pump_bg(1)

    p.dma("sync", ew_d[:, :], ewT_s[0:8, :], reads=[b_ewT])
    ph.close()
    p.barrier()

    ph = ExitStack()
    sbp = lambda name, shape, dt: ph.enter_context(nc.sbuf_tensor(name, shape, dt))
    psp = lambda name, shape, dt: ph.enter_context(nc.psum_tensor(name, shape, dt))
    ropec_s = sbp("ropec_s", [128, S], BF16); b_ropec = p.buf()
    ropes_s = sbp("ropes_s", [128, S], BF16); b_ropes = p.buf()
    xnc = [sbp(f"xnc{i}", [128, DcK * 512], BF16) for i in range(2)]
    b_xnc = [p.buf() for _ in range(2)]

    def kr_xn_load(tc):
        sl = tc % 2
        for qu in range(2):
            csl = slice(qu * 4096, (qu + 1) * 4096)
            p.dma("sync", xnc[sl][:, csl], xn_pk[tc, :, csl], writes=[b_xnc[sl]])

    kr_xn_load(0)
    p.dma("sync", ropec_s[:, :], ropec_d[:, :], writes=[b_ropec])
    p.dma("sync", ropes_s[:, :], ropes_d[:, :], writes=[b_ropes])
    wkr_s = sbp("wkr_s", [128, DcK * D], BF16)
    b_wkrm = [p.buf() for _ in range(DcK)]
    nc.sync.wait_ge(bg_sem, 16 * 4)
    stream_weight_bf(wkrb_d, wkr_s, 4, D, DcK, b_wkrm)
    krp = [sbp(f"krp{i}", [128, 512], BF16) for i in range(4)]
    b_krp = [p.buf() for _ in range(4)]
    krsw = [sbp(f"krsw{i}", [128, 512], BF16) for i in range(2)]
    b_krsw = [p.buf() for _ in range(2)]
    rta = [sbp(f"rta{i}", [128, 512], BF16) for i in range(2)]
    b_rta = [p.buf() for _ in range(2)]
    krt = [sbp(f"krt{i}", [128, 512], BF16) for i in range(2)]
    b_krt = [p.buf() for _ in range(2)]
    ps_pr = [psp(f"ps_pr{i}", [128, 512], F32) for i in range(4)]
    b_pspr = [p.buf() for _ in range(4)]
    SWAP_MASK = [i ^ 1 for i in range(32)]

    def rope_chain(src_tile, b_src, s2, tok, out_dram_ap):
        p.emit("dve", lambda v: v.stream_shuffle(krsw[s2][:, :], src_tile, SWAP_MASK),
               reads=[b_src], writes=[b_krsw[s2]])
        p.emit("dve", lambda v: v.tensor_tensor(
            rta[s2][:, :], src_tile, ropec_s[:, tok], ALU.mult),
               reads=[b_src, b_ropec], writes=[b_rta[s2]])
        p.emit("dve", lambda v: v.tensor_tensor(
            krsw[s2][:, :], krsw[s2][:, :], ropes_s[:, tok], ALU.mult),
               reads=[b_ropes], writes=[b_krsw[s2]])
        p.emit("dve", lambda v: v.tensor_add(
            krt[s2][:, :], rta[s2][:, :], krsw[s2][:, :]),
               reads=[b_rta[s2], b_krsw[s2]], writes=[b_krt[s2]])
        p.dma("sync", out_dram_ap, krt[s2][:, :], reads=[b_krt[s2]])

    for tc in range(NT):
        sl = tc % 2
        tok = slice(tc * 512, (tc + 1) * 512)
        if tc + 1 < NT:
            kr_xn_load(tc + 1)
        for m in range(DcK):
            s4 = m % 4
            s2 = m % 2
            mm_group(p, ps_pr[s4][:, :],
                     [(wkr_s[:, c * D + m * 128: c * D + (m + 1) * 128],
                       xnc[sl][:, c * 512:(c + 1) * 512]) for c in range(DcK)],
                     reads=[b_wkrm[m], b_xnc[sl]], writes=[b_pspr[s4]])
            p.emit("act", lambda s, m=m, s4=s4: s.activation(
                krp[s4][:, :], ps_pr[s4][:, :], AF.Identity,
                bias=bias_s[:, 12 + m:13 + m], scale=1.0),
                   reads=[b_pspr[s4], b_bias], writes=[b_krp[s4]])
            rope_chain(krp[s4][:, :], b_krp[s4], s2, tok,
                       kr_d[m * 128:(m + 1) * 128, tok])
        pump_bg(2)
    ph.close()
    p.barrier()

    ph = ExitStack()
    sbp = lambda name, shape, dt: ph.enter_context(nc.sbuf_tensor(name, shape, dt))
    psp = lambda name, shape, dt: ph.enter_context(nc.psum_tensor(name, shape, dt))
    ropec_s = sbp("ropec_s2", [128, S], BF16); b_ropec = p.buf()
    ropes_s = sbp("ropes_s2", [128, S], BF16); b_ropes = p.buf()
    xnc2 = [sbp(f"xnc2_{i}", [128, DcK * 512], BF16) for i in range(2)]
    b_xnc2 = [p.buf() for _ in range(2)]
    for qu in range(2):
        csl = slice(qu * 4096, (qu + 1) * 4096)
        p.dma("sync", xnc2[0][:, csl], xn_pk[0, :, csl], writes=[b_xnc2[0]])
    p.dma("sync", ropec_s[:, :], ropec_d[:, :], writes=[b_ropec])
    p.dma("sync", ropes_s[:, :], ropes_d[:, :], writes=[b_ropes])
    wdkv_s = sbp("wdkv_s", [128, DcK * DC], BF16)
    wdq_s = sbp("wdq_s", [128, DcK * DQC], BF16)
    wuk_s = sbp("wuk_s", [128, 4 * D], BF16)
    wuv_s = sbp("wuv_s", [128, 4 * D], BF16)
    wuq_s = sbp("wuq_s", [128, 8 * D], BF16)
    b_wdkvm = [p.buf() for _ in range(4)]
    b_wukm = [p.buf() for _ in range(DcK)]
    b_wuvm = [p.buf() for _ in range(4)]
    b_wdqm = [p.buf() for _ in range(8)]
    b_wuqm = [p.buf() for _ in range(DcK)]
    nc.sync.wait_ge(bg_sem, 16 * 11)
    stream_weight_bf(wdkvb_d, wdkv_s, 1, DC, DcK, b_wdkvm)
    stream_weight_bf(wukb_d, wuk_s, 4, D, 4, b_wukm)
    stream_weight_bf(wuvb_d, wuv_s, 1, D, 4, b_wuvm)
    stream_weight_bf(wdqb_d, wdq_s, 2, DQC, DcK, b_wdqm)
    stream_weight_bf(wuqb_d, wuq_s, 4, D, 8, b_wuqm)
    buvb = sbp("buvb", [128, 4 * 512], F32)
    b_buvb = p.buf()
    for n_ in range(4):
        p.dma("sync", buvb[:, n_ * 512:(n_ + 1) * 512],
              b_uv_d.rearrange("(one n) -> one n", one=1)[0:1, n_ * 512:(n_ + 1) * 512]
              .broadcast_to([128, 512]))
    p.seal("sync", b_buvb)
    ckv_c = sbp("ckv_c", [128, 4 * 512], BF16); b_ckv = p.buf()
    cq_c = sbp("cq_c", [128, 8 * 512], BF16); b_cq = p.buf()
    prj = [sbp(f"prj{i}", [128, 512], BF16) for i in range(4)]
    b_prj = [p.buf() for _ in range(4)]
    qsw = [sbp(f"qsw{i}", [128, 512], BF16) for i in range(2)]
    b_qsw = [p.buf() for _ in range(2)]
    rta2 = [sbp(f"rta2{i}", [128, 512], BF16) for i in range(2)]
    b_rta2 = [p.buf() for _ in range(2)]
    qrt = [sbp(f"qrt{i}", [128, 512], BF16) for i in range(2)]
    b_qrt = [p.buf() for _ in range(2)]
    vt = [sbp(f"vt{i}", [128, 512], BF16) for i in range(4)]
    b_vt = [p.buf() for _ in range(4)]
    ps_a = [psp(f"ps_a{i}", [128, 512], F32) for i in range(4)]
    b_psa = [p.buf() for _ in range(4)]
    ps_b = [psp(f"ps_b{i}", [128, 512], F32) for i in range(4)]
    b_psb = [p.buf() for _ in range(4)]
    SWAP_MASK2 = [i ^ 1 for i in range(32)]

    for tc in range(NT):
        sl = tc % 2
        tok = slice(tc * 512, (tc + 1) * 512)
        if tc + 1 < NT:
            for qu in range(2):
                csl = slice(qu * 4096, (qu + 1) * 4096)
                p.dma("sync", xnc2[1 - sl][:, csl], xn_pk[tc + 1, :, csl],
                      writes=[b_xnc2[1 - sl]])
        for m in range(4):
            s4 = m % 4
            mm_group(p, ps_a[s4][:, :],
                     [(wdkv_s[:, c * DC + m * 128: c * DC + (m + 1) * 128],
                       xnc2[sl][:, c * 512:(c + 1) * 512]) for c in range(DcK)],
                     reads=[b_wdkvm[m], b_xnc2[sl]], writes=[b_psa[s4]])
            p.emit("act", lambda s, m=m, s4=s4: s.activation(
                ckv_c[:, m * 512:(m + 1) * 512], ps_a[s4][:, :], AF.Identity,
                bias=bias_s[:, m:m + 1], scale=1.0),
                   reads=[b_psa[s4], b_bias], writes=[b_ckv])
        for m in range(DcK):
            s4 = m % 4
            mm_group(p, ps_a[s4][:, :],
                     [(wuk_s[:, c * D + m * 128: c * D + (m + 1) * 128],
                       ckv_c[:, c * 512:(c + 1) * 512]) for c in range(4)],
                     reads=[b_wukm[m], b_ckv], writes=[b_psa[s4]])
            p.emit("act", lambda s, m=m, s4=s4: s.activation(
                prj[s4][:, :], ps_a[s4][:, :], AF.Identity,
                bias=bias_s[:, 28 + m:29 + m], scale=1.0),
                   reads=[b_psa[s4], b_bias], writes=[b_prj[s4]])
            p.dma("sync", kc_d[m * 128:(m + 1) * 128, tok], prj[s4][:, :],
                  reads=[b_prj[s4]])
        for mt in range(4):
            for n_ in range(4):
                s4 = (mt * 4 + n_) % 4
                mm_group(p, ps_b[s4][:, :],
                         [(ckv_c[:, c * 512 + mt * 128: c * 512 + (mt + 1) * 128],
                           wuv_s[:, c * D + n_ * 512: c * D + (n_ + 1) * 512])
                          for c in range(4)],
                         reads=[b_ckv, b_wuvm[n_]], writes=[b_psb[s4]])
                p.emit("dve", lambda v, s4=s4, n_=n_: v.tensor_add(
                    vt[s4][:, :], ps_b[s4][:, :], buvb[:, n_ * 512:(n_ + 1) * 512]),
                       reads=[b_psb[s4], b_buvb], writes=[b_vt[s4]])
                p.dma("sync",
                      v_d[tc * 512 + mt * 128: tc * 512 + (mt + 1) * 128,
                          n_ * 512:(n_ + 1) * 512],
                      vt[s4][:, :], reads=[b_vt[s4]])
        for m in range(8):
            s4 = m % 4
            mm_group(p, ps_a[s4][:, :],
                     [(wdq_s[:, c * DQC + m * 128: c * DQC + (m + 1) * 128],
                       xnc2[sl][:, c * 512:(c + 1) * 512]) for c in range(DcK)],
                     reads=[b_wdqm[m], b_xnc2[sl]], writes=[b_psa[s4]])
            p.emit("act", lambda s, m=m, s4=s4: s.activation(
                cq_c[:, m * 512:(m + 1) * 512], ps_a[s4][:, :], AF.Identity,
                bias=bias_s[:, 4 + m:5 + m], scale=1.0),
                   reads=[b_psa[s4], b_bias], writes=[b_cq])
        for m in range(DcK):
            s4 = m % 4
            s2 = m % 2
            mm_group(p, ps_a[s4][:, :],
                     [(wuq_s[:, c * D + m * 128: c * D + (m + 1) * 128],
                       cq_c[:, c * 512:(c + 1) * 512]) for c in range(8)],
                     reads=[b_wuqm[m], b_cq], writes=[b_psa[s4]])
            p.emit("act", lambda s, m=m, s4=s4: s.activation(
                prj[s4][:, :], ps_a[s4][:, :], AF.Identity,
                bias=bias_s[:, 44 + m:45 + m], scale=1.0),
                   reads=[b_psa[s4], b_bias], writes=[b_prj[s4]])
            p.dma("sync", qc_d[m * 128:(m + 1) * 128, tok], prj[s4][:, :],
                  reads=[b_prj[s4]])
            p.emit("dve", lambda v, s4=s4, s2=s2: v.stream_shuffle(
                qsw[s2][:, :], prj[s4][:, :], SWAP_MASK2),
                   reads=[b_prj[s4]], writes=[b_qsw[s2]])
            p.emit("dve", lambda v, s4=s4, s2=s2: v.tensor_tensor(
                rta2[s2][:, :], prj[s4][:, :], ropec_s[:, tok], ALU.mult),
                   reads=[b_prj[s4], b_ropec], writes=[b_rta2[s2]])
            p.emit("dve", lambda v, s2=s2: v.tensor_tensor(
                qsw[s2][:, :], qsw[s2][:, :], ropes_s[:, tok], ALU.mult),
                   reads=[b_ropes], writes=[b_qsw[s2]])
            p.emit("dve", lambda v, s2=s2: v.tensor_add(
                qrt[s2][:, :], rta2[s2][:, :], qsw[s2][:, :]),
                   reads=[b_rta2[s2], b_qsw[s2]], writes=[b_qrt[s2]])
            p.dma("sync", qr_d[m * 128:(m + 1) * 128, tok], qrt[s2][:, :],
                  reads=[b_qrt[s2]])
        pump_bg(2)
    ph.close()
    p.barrier()

    ph = ExitStack()
    sbp = lambda name, shape, dt: ph.enter_context(nc.sbuf_tensor(name, shape, dt))
    psp = lambda name, shape, dt: ph.enter_context(nc.psum_tensor(name, shape, dt))
    masks_s = sbp("masks_s", [128, 4 * 512], F32); b_masks = p.buf()
    p.dma("sync", masks_s[:, :], masks_d[:, :], writes=[b_masks])
    kch = [sbp(f"kch{i}", [128, S], BF16) for i in range(2)]
    b_kch = [p.buf() for _ in range(2)]
    krh = [sbp(f"krh{i}", [128, S], BF16) for i in range(2)]
    b_krh = [p.buf() for _ in range(2)]
    qch = [sbp(f"qch{i}", [128, S], BF16) for i in range(2)]
    b_qch = [p.buf() for _ in range(2)]
    qrh = [sbp(f"qrh{i}", [128, S], BF16) for i in range(2)]
    b_qrh = [p.buf() for _ in range(2)]
    vh = [sbp(f"vh{i}", [128, S], BF16) for i in range(2)]
    b_vh = [p.buf() for _ in range(2)]
    ohp = [sbp(f"ohp{i}", [128, 512], BF16) for i in range(2)]
    b_ohp = [p.buf() for _ in range(2)]
    ptb = [sbp(f"ptb{i}", [128, (S // 128) * 512], BF16) for i in range(2)]
    b_ptb = [[p.buf() for _ in range(S // 128)] for _ in range(2)]
    srow = [sbp(f"srowa{i}", [1, 512], F32) for i in range(2)]
    b_srowa = [p.buf() for _ in range(2)]
    sumb = [sbp(f"sumb{i}", [128, 512], F32) for i in range(2)]
    b_sumb = [p.buf() for _ in range(2)]
    inv_d = nc.dram_tensor("inv_d", [2, 512], F32)
    b_invd = [p.buf() for _ in range(2)]
    ps_t = [psp(f"ps_t{i}", [128, 512], F32) for i in range(4)]
    b_pst = [p.buf() for _ in range(4)]
    ps_sum = [psp(f"ps_sum{i}", [128, 512], F32) for i in range(2)]
    b_pssum = [p.buf() for _ in range(2)]
    ps_o = [psp(f"ps_o{i}", [128, 512], F32) for i in range(2)]
    b_pso = [p.buf() for _ in range(2)]

    def attn_loads_kv(h, sl):
        hs = slice(h * 128, (h + 1) * 128)
        p.dma("sync", kch[sl][:, :], kc_d[hs, :], writes=[b_kch[sl]])
        p.dma("sync", krh[sl][:, :], kr_d[hs, :], writes=[b_krh[sl]])
        p.dma("sync", qch[sl][:, :], qc_d[hs, :], writes=[b_qch[sl]])
        p.dma("sync", qrh[sl][:, :], qr_d[hs, :], writes=[b_qrh[sl]])
        p.dma("sync", vh[sl].rearrange("q (t n) -> q t n", n=128),
              v_d.rearrange("(t q) d -> q t d", q=128)[:, :, hs], writes=[b_vh[sl]])

    rr = {"t": 0}
    pend = []

    def flush():
        while pend:
            pend.pop(0)()

    def make_sum_pv(h, qs, sl):
        ntk = (qs + 1) * 4
        sp = qs % 2

        def sum_pv():
            def smbuild(t):
                last = None
                for tkc in range(ntk):
                    last = t.matmul(ps_sum[sp][0:1, :], ones_c[:, :],
                                    ptb[sp][:, tkc * 512:(tkc + 1) * 512],
                                    start=(tkc == 0), stop=(tkc == ntk - 1))
                return last
            p.emit("pe", smbuild,
                   reads=[b_ones] + [b_ptb[sp][tkc] for tkc in range(ntk)],
                   writes=[b_pssum[sp]])
            p.emit("dve", lambda v: v.tensor_copy(srow[sp][0:1, :],
                                                  ps_sum[sp][0:1, :]),
                   reads=[b_pssum[sp]], writes=[b_srowa[sp]])
            p.emit("dve", lambda v: v.reciprocal(srow[sp][0:1, :], srow[sp][0:1, :]),
                   writes=[b_srowa[sp]])
            p.dma("sync", inv_d[sp:sp + 1, :], srow[sp][0:1, :],
                  reads=[b_srowa[sp]], writes=[b_invd[sp]])
            p.dma("sync", sumb[sp][:, :],
                  inv_d[sp:sp + 1, :].broadcast_to([128, 512]),
                  reads=[b_invd[sp]], writes=[b_sumb[sp]])
            mm_group(p, ps_o[sp][:, :],
                     [(vh[sl][:, tkc * 128:(tkc + 1) * 128],
                       ptb[sp][:, tkc * 512:(tkc + 1) * 512]) for tkc in range(ntk)],
                     reads=[b_vh[sl]] + [b_ptb[sp][tkc] for tkc in range(ntk)],
                     writes=[b_pso[sp]])
        return sum_pv

    def make_odiv(h, qs, sl):
        sp = qs % 2

        def odiv():
            p.emit("dve", lambda v: v.tensor_mul(ohp[sp][:, :], ps_o[sp][:, :],
                                                 sumb[sp][:, :]),
                   reads=[b_pso[sp], b_sumb[sp]], writes=[b_ohp[sp]])
            p.dma("sync", o_d[h * 128:(h + 1) * 128, qs * 512:(qs + 1) * 512],
                  ohp[sp][:, :], reads=[b_ohp[sp]])
        return odiv

    attn_loads_kv(0, 0)
    for h in range(H):
        sl = h % 2
        for qs in range(NQS):
            if qs == 1 and h + 1 < H:
                attn_loads_kv(h + 1, (h + 1) % 2)
                pump_bg(3)
            ntk = (qs + 1) * 4
            sp = qs % 2
            qsl5 = slice(qs * 512, (qs + 1) * 512)
            for tkc in range(ntk):
                bank = rr["t"] % 4
                rr["t"] += 1
                tsl = slice(tkc * 128, (tkc + 1) * 128)
                mm_group(p, ps_t[bank][:, :],
                         [(kch[sl][:, tsl], qch[sl][:, qsl5]),
                          (krh[sl][:, tsl], qrh[sl][:, qsl5])],
                         reads=[b_kch[sl], b_krh[sl], b_qch[sl], b_qrh[sl]],
                         writes=[b_pst[bank]])
                i = tkc - qs * 4
                if i >= 0:
                    p.emit("dve", lambda v, bank=bank, i=i: v.tensor_add(
                        ps_t[bank][:, :], ps_t[bank][:, :],
                        masks_s[:, i * 512:(i + 1) * 512]),
                           reads=[b_masks], writes=[b_pst[bank]])
                p.emit("act", lambda s, bank=bank, tkc=tkc, sp=sp: s.activation(
                    ptb[sp][:, tkc * 512:(tkc + 1) * 512], ps_t[bank][:, :], AF.Exp,
                    bias=0.0, scale=1.0 / 16.0),
                       reads=[b_pst[bank]], writes=[b_ptb[sp][tkc]])
                if tkc == 3 and pend:
                    pend.pop(0)()
                if tkc == 7 and pend:
                    pend.pop(0)()
            pend.append(make_sum_pv(h, qs, sl))
            pend.append(make_odiv(h, qs, sl))
            while len(pend) > 2:
                pend.pop(0)()
    flush()
    ph.close()
    p.barrier()

    ph = ExitStack()
    sbp = lambda name, shape, dt: ph.enter_context(nc.sbuf_tensor(name, shape, dt))
    psp = lambda name, shape, dt: ph.enter_context(nc.psum_tensor(name, shape, dt))
    oc = [sbp(f"oc{i}", [128, DcK * 512], BF16) for i in range(2)]
    b_oc = [p.buf() for _ in range(2)]
    xc2 = [sbp(f"xc2{i}", [128, DcK * 512], F32) for i in range(2)]
    b_xc2 = [p.buf() for _ in range(2)]
    t2 = [sbp(f"t2{i}", [128, 512], F32) for i in range(4)]
    b_t2 = [p.buf() for _ in range(4)]
    ps_w = [psp(f"ps_w{i}", [128, 512], F32) for i in range(4)]
    b_psw2 = [p.buf() for _ in range(4)]

    def wo_loads(tc):
        sl = tc % 2
        tok = slice(tc * 512, (tc + 1) * 512)
        p.dma("sync", oc[sl].rearrange("q (c n) -> q c n", n=512),
              o_d.rearrange("(c q) s -> q c s", q=128)[:, :, tok], writes=[b_oc[sl]])
        for qu in range(4):
            csl = slice(qu * 2048, (qu + 1) * 2048)
            p.dma("sync", xc2[sl][:, csl], x_pk[tc, :, csl], writes=[b_xc2[sl]])

    wo_loads(0)
    wo_s = sbp("wo_s", [128, DcK * D], BF16)
    b_wom = [p.buf() for _ in range(DcK)]
    nc.sync.wait_ge(bg_sem, 16 * N_BG_ATTN)
    stream_weight_bf(wob_d, wo_s, 4, D, DcK, b_wom)
    for tc in range(NT):
        sl = tc % 2
        tok = slice(tc * 512, (tc + 1) * 512)
        for m in range(DcK):
            if m == 8 and tc + 1 < NT:
                wo_loads(tc + 1)
            s4 = m % 4
            mm_group(p, ps_w[s4][:, :],
                     [(wo_s[:, c * D + m * 128: c * D + (m + 1) * 128],
                       oc[sl][:, c * 512:(c + 1) * 512]) for c in range(DcK)],
                     reads=[b_wom[m], b_oc[sl]], writes=[b_psw2[s4]])
            p.emit("act", lambda s, m=m, s4=s4: s.activation(
                t2[s4][:, :], ps_w[s4][:, :], AF.Identity,
                bias=bias_s[:, 60 + m:61 + m], scale=1.0),
                   reads=[b_psw2[s4], b_bias], writes=[b_t2[s4]])
            p.emit("dve", lambda v, m=m, s4=s4: v.tensor_add(
                t2[s4][:, :], t2[s4][:, :],
                xc2[sl][:, m * 512:(m + 1) * 512]),
                   reads=[b_xc2[sl]], writes=[b_t2[s4]])
            p.dma("sync", part_d[m * 128:(m + 1) * 128, tok], t2[s4][:, :],
                  reads=[b_t2[s4]])
    ph.close()
    p.barrier()

    ph = ExitStack()
    sbp = lambda name, shape, dt: ph.enter_context(nc.sbuf_tensor(name, shape, dt))
    psp = lambda name, shape, dt: ph.enter_context(nc.psum_tensor(name, shape, dt))
    xnf = sbp("xnf", [128, DcK * CF], BF16); b_xnf = p.buf()
    acc = sbp("acc", [128, DcK * CF], F32)
    b_accm = [p.buf() for _ in range(DcK)]
    g_s = sbp("g_s", [128, 8 * CF], BF16)
    b_gs = [p.buf() for _ in range(8)]
    w13 = [sbp(f"w13{i}", [128, DcK * 256], BF16) for i in range(4)]
    b_w13 = [p.buf() for _ in range(4)]
    w2s = [sbp(f"w2s{i}", [128, 8 * 256], BF16) for i in range(2)]
    b_w2s = [p.buf() for _ in range(2)]
    ewb = [sbp(f"ewb{i}", [128, CF], BF16) for i in range(2)]
    b_ewb = [p.buf() for _ in range(2)]
    silt = [sbp(f"silt{i}", [128, 512], BF16) for i in range(2)]
    b_sil = [p.buf() for _ in range(2)]
    h3t = [sbp(f"h3t{i}", [128, 512], BF16) for i in range(2)]
    b_h3 = [p.buf() for _ in range(2)]
    gtmp = [sbp(f"gtmp{i}", [128, 512], BF16) for i in range(2)]
    b_gt = [p.buf() for _ in range(2)]
    ps_1 = [psp(f"ps_1{i}", [128, 512], F32) for i in range(2)]
    b_ps1 = [p.buf() for _ in range(2)]
    ps_3 = [psp(f"ps_3{i}", [128, 512], F32) for i in range(2)]
    b_ps3 = [p.buf() for _ in range(2)]
    ps_2 = [psp(f"ps_2{i}", [128, 512], F32) for i in range(2)]
    b_ps2 = [p.buf() for _ in range(2)]

    pump_bg(len(bg_jobs))
    nc.sync.wait_ge(bg_sem, 16 * len(bg_jobs))

    w1b_r = w1b_d.rearrange("e (c q) n -> e q c n", q=128)
    w3b_r = w3b_d.rearrange("e (c q) n -> e q c n", q=128)
    w2b_r = w2b_d.rearrange("e (c q) n -> e q c n", q=128)

    part_r = part_d.rearrange("(c q) s -> q c s", q=128)
    out_r = out_fm.rearrange("(c q) s -> q c s", q=128)
    for cf in range(NFC):
        tok = slice(cf * CF, (cf + 1) * CF)
        if cf == 0:
            for nt in range(NTC):
                p.dma("sync", xnf[:, nt * 8192:(nt + 1) * 8192],
                      xn_pk[cf * NTC + nt, :, :], writes=[b_xnf])
        for e_ in range(E):
            se_ = e_ % 2
            p.dma("sync", ewb[se_][:, :],
                  ew_d[e_:e_ + 1, tok].broadcast_to([128, CF]), writes=[b_ewb[se_]])
            for fc2 in range(4):
                w1sl = w13[(fc2 % 2) * 2]
                w3sl = w13[(fc2 % 2) * 2 + 1]
                bw1 = b_w13[(fc2 % 2) * 2]
                bw3 = b_w13[(fc2 % 2) * 2 + 1]
                fsl = slice(fc2 * 256, (fc2 + 1) * 256)
                if not (cf > 0 and e_ == 0 and fc2 == 0):
                    p.dma("sync", w1sl.rearrange("q (c n) -> q c n", n=256),
                          w1b_r[e_, :, :, fsl], writes=[bw1])
                    p.dma("sync", w3sl.rearrange("q (c n) -> q c n", n=256),
                          w3b_r[e_, :, :, fsl], writes=[bw3])
                if cf == 0 and e_ == 0 and fc2 == 1:
                    for m in range(DcK):
                        p.dma("sync" if m % 2 == 0 else "gdma",
                              acc[:, m * CF:(m + 1) * CF], part_r[:, m, tok],
                              writes=[b_accm[m]])
                for sm in range(2):
                    fm = fc2 * 2 + sm
                    for nt in range(NTC):
                        s2 = nt % 2
                        tsl = slice(nt * 512, (nt + 1) * 512)
                        mm_group(p, ps_1[s2][:, :],
                                 [(w1sl[:, c * 256 + sm * 128: c * 256 + (sm + 1) * 128],
                                   xnf[:, nt * 8192 + c * 512: nt * 8192 + (c + 1) * 512])
                                  for c in range(DcK)],
                                 reads=[bw1, b_xnf], writes=[b_ps1[s2]])
                        mm_group(p, ps_3[s2][:, :],
                                 [(w3sl[:, c * 256 + sm * 128: c * 256 + (sm + 1) * 128],
                                   xnf[:, nt * 8192 + c * 512: nt * 8192 + (c + 1) * 512])
                                  for c in range(DcK)],
                                 reads=[bw3, b_xnf], writes=[b_ps3[s2]])
                        p.emit("act", lambda s, s2=s2: s.activation(
                            silt[s2][:, :], ps_1[s2][:, :], AF.Silu),
                               reads=[b_ps1[s2]], writes=[b_sil[s2]])
                        p.emit("dve", lambda v, s2=s2, tsl=tsl: v.tensor_mul(
                            h3t[s2][:, :], ps_3[s2][:, :], ewb[se_][:, tsl]),
                               reads=[b_ps3[s2], b_ewb[se_]], writes=[b_h3[s2]])
                        p.emit("dve", lambda v, s2=s2, fm=fm, nt=nt: v.tensor_mul(
                            g_s[:, fm * CF + nt * 512: fm * CF + (nt + 1) * 512],
                            silt[s2][:, :], h3t[s2][:, :]),
                               reads=[b_sil[s2], b_h3[s2]], writes=[b_gs[fm]])
            for m2 in range(8):
                sw = m2 % 2
                msl = slice(m2 * 256, (m2 + 1) * 256)
                p.dma("sync", w2s[sw].rearrange("q (c n) -> q c n", n=256),
                      w2b_r[e_, :, :, msl], writes=[b_w2s[sw]])
                if e_ == E - 1 and m2 == 0 and cf + 1 < NFC:
                    for nt in range(NTC):
                        p.dma("gdma", xnf[:, nt * 8192:(nt + 1) * 8192],
                              xn_pk[(cf + 1) * NTC + nt, :, :], writes=[b_xnf])
                if e_ == E - 1 and m2 == 4 and cf + 1 < NFC:
                    p.dma("sync", w13[0].rearrange("q (c n) -> q c n", n=256),
                          w1b_r[0, :, :, 0:256], writes=[b_w13[0]])
                    p.dma("sync", w13[1].rearrange("q (c n) -> q c n", n=256),
                          w3b_r[0, :, :, 0:256], writes=[b_w13[1]])
                for sm in range(2):
                    m = m2 * 2 + sm
                    for nt in range(NTC):
                        s2 = nt % 2
                        mm_group(p, ps_2[s2][:, :],
                                 [(w2s[sw][:, c * 256 + sm * 128: c * 256 + (sm + 1) * 128],
                                   g_s[:, c * CF + nt * 512: c * CF + (nt + 1) * 512])
                                  for c in range(8)],
                                 reads=[b_w2s[sw]] + b_gs, writes=[b_ps2[s2]])
                        p.emit("dve", lambda v, s2=s2, m=m, nt=nt: v.tensor_add(
                            acc[:, m * CF + nt * 512: m * CF + (nt + 1) * 512],
                            acc[:, m * CF + nt * 512: m * CF + (nt + 1) * 512],
                            ps_2[s2][:, :]),
                               reads=[b_ps2[s2]], writes=[b_accm[m]])
                    if e_ == E - 1:
                        p.dma("sync" if m % 2 == 0 else "gdma",
                              out_r[:, m, tok], acc[:, m * CF:(m + 1) * CF],
                              reads=[b_accm[m]])
                        if cf + 1 < NFC:
                            ntok = slice((cf + 1) * CF, (cf + 2) * CF)
                            p.dma("sync" if m % 2 == 1 else "gdma",
                                  acc[:, m * CF:(m + 1) * CF], part_r[:, m, ntok],
                                  writes=[b_accm[m]])
    for q, n in DMA_POOL.items():
        eng = nc.sync if q == "sync" else nc.gpsimd
        for j in range(n):
            key = f"{q}{j}"
            if p.cnt[key] > 0:
                eng.wait_ge(p.sems[key], p.cnt[key])
    ph.close()
    stack.close()
    nc.finalize()
    return nc



def _host_consts(S):
    bf = ml_dtypes.bfloat16
    ident = np.eye(128)
    pswap = np.zeros((128, 128), dtype=np.float32)
    for k in range(128):
        pswap[k, k ^ 1] = 1.0
    ones_col = np.ones((128, 1), dtype=np.float32)
    idx = np.arange(64, dtype=np.float32)
    freqs = np.float32(10000.0) ** (np.float32(-2.0) * idx / np.float32(128.0))
    pos = np.arange(S, dtype=np.float32)
    theta = pos[:, None] * freqs[None, :]
    cosv = np.cos(theta.astype(np.float64)).astype(np.float32)
    sinv = np.sin(theta.astype(np.float64)).astype(np.float32)
    ropec = np.repeat(cosv.T, 2, axis=0)
    base = np.repeat(sinv.T, 2, axis=0)
    sign = np.where(np.arange(128) % 2 == 0, -1.0, 1.0).astype(np.float32)
    ropes = base * sign[:, None]
    masks = np.zeros((128, 4 * 512), dtype=np.float32)
    j = np.arange(512)
    for i in range(4):
        prow = i * 128 + np.arange(128)
        masks[:, i * 512:(i + 1) * 512] = np.where(
            j[None, :] >= prow[:, None], 0.0, NEG).astype(np.float32)
    iota = np.tile(np.arange(8, dtype=np.float32), (128, 4))
    return {
        "ident_bf": ident.astype(bf), "ident_f32": ident.astype(np.float32),
        "pswap": pswap.astype(bf), "ones_col": ones_col.astype(bf),
        "ropec": ropec.astype(bf), "ropes": ropes.astype(bf),
        "masks": masks, "iota8": iota,
        "iota8m": (iota - 99.0).astype(np.float32),
    }


def host_inputs(g, S):
    f32 = np.float32
    shared = {
        "wdkv_t": np.ascontiguousarray(g["L0_attn_W_DKV_w"].T.astype(f32)),
        "wuk_t": np.ascontiguousarray(g["L0_attn_W_UK_w"].T.astype(f32)),
        "wuv_t": np.ascontiguousarray(g["L0_attn_W_UV_w"].T.astype(f32)),
        "wdq_t": np.ascontiguousarray(g["L0_attn_W_DQ_w"].T.astype(f32)),
        "wuq_t": np.ascontiguousarray(g["L0_attn_W_UQ_w"].T.astype(f32)),
        "wkr_t": np.ascontiguousarray(g["L0_attn_W_KR_w"].T.astype(f32)),
        "wo_t": np.ascontiguousarray(g["L0_attn_W_O_w"].T.astype(f32)),
        "wg_t": np.ascontiguousarray(g["L0_ffn_Wg"].T.astype(f32)),
        "w1_t": np.ascontiguousarray(np.transpose(g["L0_ffn_W1"], (0, 2, 1)).astype(f32)),
        "w3_t": np.ascontiguousarray(np.transpose(g["L0_ffn_W3"], (0, 2, 1)).astype(f32)),
        "w2_t": np.ascontiguousarray(np.transpose(g["L0_ffn_W2"], (0, 2, 1)).astype(f32)),
        "b_uv": np.ascontiguousarray(g["L0_attn_W_UV_b"].astype(f32)),
        "fbias": np.ascontiguousarray(g["L0_ffn_bias"].astype(f32)),
    }
    bias_pk = np.zeros((128, 76), dtype=f32)
    for key, col0, n in (("L0_attn_W_DKV_b", 0, 4), ("L0_attn_W_DQ_b", 4, 8),
                         ("L0_attn_W_KR_b", 12, 16), ("L0_attn_W_UK_b", 28, 16),
                         ("L0_attn_W_UQ_b", 44, 16), ("L0_attn_W_O_b", 60, 16)):
        bias_pk[:, col0:col0 + n] = g[key].astype(f32).reshape(n, 128).T
    shared["bias_pk"] = np.ascontiguousarray(bias_pk)
    shared.update(_host_consts(S))
    return shared


def kernel(**inputs):
    g = {k: np.asarray(v) for k, v in inputs.items()}
    x = g["L0_attn_h"]
    B, S_, D_ = x.shape
    assert D_ == D and B == NCORE
    shared = host_inputs(g, S_)
    in_maps = []
    NT_ = S_ // 512
    for b in range(B):
        m = dict(shared)
        xb_ = x[b].astype(np.float32).reshape(NT_, 512, DcK, 128)
        m["x_pk"] = np.ascontiguousarray(
            xb_.transpose(0, 3, 2, 1).reshape(NT_, 128, DcK * 512))
        in_maps.append(m)
    nc = build_program(S=S_)
    res = run_bass_kernel_spmd(nc, in_maps, core_ids=list(range(NCORE)))
    out = np.empty((B, S_, D), dtype=np.float32)
    for b in range(B):
        out[b] = np.asarray(res.results[b]["out_fm"]).T
    return out


# version 23
# speedup vs baseline: 1.1787x; 1.1787x over previous
import numpy as np
import ml_dtypes
import concourse.bass as bass
import concourse.mybir as mybir
from concourse.bass_utils import run_bass_kernel_spmd
from contextlib import ExitStack

F32 = mybir.dt.float32
BF16 = mybir.dt.bfloat16
AF = mybir.ActivationFunctionType
ALU = mybir.AluOpType
AX = mybir.AxisListType

D = 2048
DC = 512
DQC = 1024
H = 16
DH = 128
FF = 1024
E = 8
NCORE = 8
NEG = -1.0e9
EPS = 1e-5
DcK = D // 128

STREAM_OF = {"sync": "sync", "gdma": "gpsimd", "pool": "gpsimd",
             "pe": "tensor", "dve": "vector", "act": "scalar"}
SEM_KEYS = ("pool", "pe", "dve", "act")
DMA_POOL = {"sync": 12, "gdma": 4}


class Buf:
    __slots__ = ("w", "r")

    def __init__(self):
        self.w = []
        self.r = []


class Prog:
    def __init__(self, nc, stack):
        self.nc = nc
        self.stack = stack
        self.eng = {"sync": nc.sync, "gpsimd": nc.gpsimd, "tensor": nc.tensor,
                    "vector": nc.vector, "scalar": nc.scalar}
        self.sems = {}
        self.cnt = {}
        self.seen = {s: {} for s in self.eng}
        self.bufs = []
        self.pnum = 0
        self._mk_sems()

    def _mk_sems(self):
        self.pnum += 1
        self.sems = {}
        self.cnt = {}
        for key in SEM_KEYS:
            self.sems[key] = self.stack.enter_context(
                self.nc.semaphore(f"p{self.pnum}_{key}"))
            self.cnt[key] = 0
        for q, n in DMA_POOL.items():
            for j in range(n):
                key = f"{q}{j}"
                self.sems[key] = self.stack.enter_context(
                    self.nc.semaphore(f"p{self.pnum}_{key}"))
                self.cnt[key] = 0
        self.rr = {q: 0 for q in DMA_POOL}
        for s in self.seen:
            self.seen[s] = {}

    def buf(self):
        b = Buf()
        self.bufs.append(b)
        return b

    def barrier(self):
        final = {k: v for k, v in self.cnt.items() if v > 0}
        for s, e in self.eng.items():
            for k, v in final.items():
                if self.seen[s].get(k, 0) < v:
                    e.wait_ge(self.sems[k], v)
                    self.seen[s][k] = v
        for b in self.bufs:
            b.w = []
            b.r = []

    def emit(self, key, build, reads=(), writes=(), deps=()):
        stream = STREAM_OF[key[:4] if key[:4] in ("sync", "gdma") else key]
        e = self.eng[stream]
        toks = []
        for b in reads:
            toks.extend(b.w)
        for b in writes:
            toks.extend(b.w)
            toks.extend(b.r)
        toks.extend(deps)
        for (k, v) in toks:
            if stream == "tensor" and k == "pe":
                continue
            if self.seen[stream].get(k, 0) < v:
                e.wait_ge(self.sems[k], v)
                self.seen[stream][k] = v
        inst = build(e)
        inc = 16 if key[:4] in ("sync", "gdma") else 1
        inst.then_inc(self.sems[key], inc)
        self.cnt[key] += inc
        tok = (key, self.cnt[key])
        for b in reads:
            b.r.append(tok)
        for b in writes:
            b.w = [tok]
            b.r = []
        return tok

    def dma(self, key, dst, src, reads=(), writes=(), deps=()):
        q = key
        j = self.rr[q] % DMA_POOL[q]
        self.rr[q] += 1
        slot = f"{q}{j}"
        deps = list(deps)
        if self.cnt[slot] > 0:
            deps.append((slot, self.cnt[slot]))
        return self.emit(slot, lambda e: e.dma_start(dst, src),
                         reads=reads, writes=writes, deps=deps)

    def dma_split(self, q, pieces, buf, deps=()):
        old = list(buf.w) + list(buf.r) + list(deps)
        toks = [self.dma(q, dst, src, deps=old) for dst, src in pieces]
        buf.w = toks
        buf.r = []

    def seal(self, q, buf):
        buf.w = [(f"{q}{j}", self.cnt[f"{q}{j}"]) for j in range(DMA_POOL[q])
                 if self.cnt[f"{q}{j}"] > 0]
        buf.r = []


def mm_group(p, psum_ap, pairs, reads, writes):
    def build(t):
        last = None
        n = len(pairs)
        for i, (l, r) in enumerate(pairs):
            last = t.matmul(psum_ap, l, r, start=(i == 0), stop=(i == n - 1))
        return last
    return p.emit("pe", build, reads=reads, writes=writes)


def build_program(S=4096, debug=()):
    assert S % 512 == 0
    NT = S // 512
    CF = min(1024, S)
    NFC = S // CF
    NTC = CF // 512
    NQS = S // 512

    nc = bass.Bass("TRN2", target_bir_lowering=False)
    dbg = lambda name: "ExternalOutput" if name in debug else "Internal"

    x_pk = nc.dram_tensor("x_pk", [S // 512, 128, DcK * 512], F32,
                          kind="ExternalInput")
    wdkv_t = nc.dram_tensor("wdkv_t", [D, DC], F32, kind="ExternalInput")
    wuk_t = nc.dram_tensor("wuk_t", [DC, D], F32, kind="ExternalInput")
    wuv_t = nc.dram_tensor("wuv_t", [DC, D], F32, kind="ExternalInput")
    wdq_t = nc.dram_tensor("wdq_t", [D, DQC], F32, kind="ExternalInput")
    wuq_t = nc.dram_tensor("wuq_t", [DQC, D], F32, kind="ExternalInput")
    wkr_t = nc.dram_tensor("wkr_t", [D, D], F32, kind="ExternalInput")
    wo_t = nc.dram_tensor("wo_t", [D, D], F32, kind="ExternalInput")
    wg_t = nc.dram_tensor("wg_t", [D, E], F32, kind="ExternalInput")
    w1_t = nc.dram_tensor("w1_t", [E, D, FF], F32, kind="ExternalInput")
    w3_t = nc.dram_tensor("w3_t", [E, D, FF], F32, kind="ExternalInput")
    w2_t = nc.dram_tensor("w2_t", [E, FF, D], F32, kind="ExternalInput")
    b_uv_d = nc.dram_tensor("b_uv", [D], F32, kind="ExternalInput")
    bias_pk_d = nc.dram_tensor("bias_pk", [128, 76], F32, kind="ExternalInput")
    fbias_d = nc.dram_tensor("fbias", [E], F32, kind="ExternalInput")
    ident_bf_d = nc.dram_tensor("ident_bf", [128, 128], BF16, kind="ExternalInput")
    ident_f32_d = nc.dram_tensor("ident_f32", [128, 128], F32, kind="ExternalInput")
    pswap_d = nc.dram_tensor("pswap", [128, 128], BF16, kind="ExternalInput")
    ones_col_d = nc.dram_tensor("ones_col", [128, 1], BF16, kind="ExternalInput")
    ropec_d = nc.dram_tensor("ropec", [128, S], BF16, kind="ExternalInput")
    ropes_d = nc.dram_tensor("ropes", [128, S], BF16, kind="ExternalInput")
    masks_d = nc.dram_tensor("masks", [128, 4 * 512], F32, kind="ExternalInput")
    iota8_d = nc.dram_tensor("iota8", [128, 32], F32, kind="ExternalInput")
    iota8m_d = nc.dram_tensor("iota8m", [128, 32], F32, kind="ExternalInput")

    out_fm = nc.dram_tensor("out_fm", [D, S], F32, kind="ExternalOutput")
    xn_pk = nc.dram_tensor("xn_pk", [S // 512, 128, DcK * 512], BF16)
    kc_d = nc.dram_tensor("kc_d", [D, S], BF16, kind=dbg("kc_d"))
    kr_d = nc.dram_tensor("kr_d", [D, S], BF16, kind=dbg("kr_d"))
    qc_d = nc.dram_tensor("qc_d", [D, S], BF16, kind=dbg("qc_d"))
    qr_d = nc.dram_tensor("qr_d", [D, S], BF16, kind=dbg("qr_d"))
    v_d = nc.dram_tensor("v_d", [S, D], BF16, kind=dbg("v_d"))
    o_d = nc.dram_tensor("o_d", [D, S], BF16, kind=dbg("o_d"))
    part_d = nc.dram_tensor("part_d", [D, S], F32, kind=dbg("part_d"))
    ew_d = nc.dram_tensor("ew_d", [E, S], BF16, kind=dbg("ew_d"))
    rows_d = nc.dram_tensor("rows_d", [NT, 2, 512], BF16)
    w1b_d = nc.dram_tensor("w1b_d", [E, D, FF], BF16)
    w3b_d = nc.dram_tensor("w3b_d", [E, D, FF], BF16)
    w2b_d = nc.dram_tensor("w2b_d", [E, FF, D], BF16)
    wkrb_d = nc.dram_tensor("wkrb_d", [D, D], BF16)
    wdkvb_d = nc.dram_tensor("wdkvb_d", [D, DC], BF16)
    wukb_d = nc.dram_tensor("wukb_d", [DC, D], BF16)
    wuvb_d = nc.dram_tensor("wuvb_d", [DC, D], BF16)
    wdqb_d = nc.dram_tensor("wdqb_d", [D, DQC], BF16)
    wuqb_d = nc.dram_tensor("wuqb_d", [DQC, D], BF16)
    wob_d = nc.dram_tensor("wob_d", [D, D], BF16)

    stack = ExitStack()
    p = Prog(nc, stack)
    sb = lambda name, shape, dt: stack.enter_context(nc.sbuf_tensor(name, shape, dt))

    id_bf = sb("id_bf", [128, 128], BF16)
    id_f = sb("id_f", [128, 128], F32)
    psw = sb("psw", [128, 128], BF16)
    ones_c = sb("ones_c", [128, 1], BF16)
    ewT_s = sb("ewT_s", [8, S], BF16)
    bias_s = sb("bias_s", [128, 76], F32)
    b_id = p.buf(); b_idf = p.buf(); b_psw = p.buf(); b_ones = p.buf()
    b_ewT = p.buf(); b_bias = p.buf()

    p.dma("gdma", id_bf[:, :], ident_bf_d[:, :], writes=[b_id])
    p.dma("sync", id_f[:, :], ident_f32_d[:, :], writes=[b_idf])
    p.dma("gdma", psw[:, :], pswap_d[:, :], writes=[b_psw])
    p.dma("gdma", ones_c[:, :], ones_col_d[:, :], writes=[b_ones])
    p.dma("sync", bias_s[:, 0:76], bias_pk_d[:, :])
    p.seal("sync", b_bias)

    bg_sem = stack.enter_context(nc.semaphore("bg_conv"))
    bg_jobs = []
    def add_cast(dst, src, rows):
        step = max(rows // max((rows * dst.shape[-1] * 4) // (4 << 20), 1), 1)
        r = 0
        while r < rows:
            r2 = min(r + step, rows)
            bg_jobs.append((dst[r:r2, :], src[r:r2, :], bg_sem))
            r = r2

    add_cast(wkrb_d, wkr_t, D)
    add_cast(wdkvb_d, wdkv_t, D)
    add_cast(wukb_d, wuk_t, DC)
    add_cast(wuvb_d, wuv_t, DC)
    add_cast(wdqb_d, wdq_t, D)
    add_cast(wuqb_d, wuq_t, DQC)
    add_cast(wob_d, wo_t, D)
    N_BG_ATTN = len(bg_jobs)
    for e_ in range(E):
        for (src, dst) in ((w1_t, w1b_d), (w3_t, w3b_d)):
            bg_jobs.append((dst[e_, 0:D // 2, :], src[e_, 0:D // 2, :], bg_sem))
            bg_jobs.append((dst[e_, D // 2:D, :], src[e_, D // 2:D, :], bg_sem))
        bg_jobs.append((w2b_d[e_, 0:FF // 2, :], w2_t[e_, 0:FF // 2, :], bg_sem))
        bg_jobs.append((w2b_d[e_, FF // 2:FF, :], w2_t[e_, FF // 2:FF, :], bg_sem))
    bg_state = {"i": 0}

    def pump_bg(n=2):
        while n > 0 and bg_state["i"] < len(bg_jobs):
            dst, src, sem = bg_jobs[bg_state["i"]]
            nc.gpsimd.dma_start(dst, src).then_inc(sem, 16)
            bg_state["i"] += 1
            n -= 1

    def stream_weight_bf(w_src_bf, w_sbuf, ncols, width, nK, bufs):
        src_r = w_src_bf.rearrange("(c q) n -> q c n", q=128)
        dst_r = w_sbuf.rearrange("q (c n) -> q c n", n=width)
        for c in range(nK):
            p.dma("sync", dst_r[:, c, :], src_r[:, c, :])
        for b in bufs:
            p.seal("sync", b)

    N_BG_WKR = 4
    pump_bg(N_BG_WKR)
    p.barrier()

    ph = ExitStack()
    sbp = lambda name, shape, dt: ph.enter_context(nc.sbuf_tensor(name, shape, dt))
    psp = lambda name, shape, dt: ph.enter_context(nc.psum_tensor(name, shape, dt))
    xc = [sbp(f"xc{i}", [128, DcK * 512], F32) for i in range(2)]
    b_xc = [p.buf() for _ in range(2)]
    xb = [sbp(f"xb{i}", [128, DcK * 512], BF16) for i in range(2)]
    b_xb = [p.buf() for _ in range(2)]
    xsq = [sbp(f"xsq{i}", [128, DcK * 512], BF16) for i in range(2)]
    b_xsq = [p.buf() for _ in range(2)]
    xnb = [sbp(f"xnb{i}", [128, DcK * 512], BF16) for i in range(2)]
    b_xnb = [p.buf() for _ in range(2)]
    srow = sbp("srow", [1, 512], F32); b_srow = p.buf()
    sqrow = sbp("sqrow", [1, 512], F32); b_sqrow = p.buf()
    varrow = sbp("varrow", [1, 512], F32); b_varrow = p.buf()
    rstd_r = sbp("rstd_r", [1, 512], BF16); b_rstd = p.buf()
    negmr_r = sbp("negmr_r", [1, 512], BF16); b_negmr = p.buf()
    rstd_b = [sbp(f"rstd_b{i}", [128, 512], BF16) for i in range(2)]
    b_rstdb = [p.buf() for _ in range(2)]
    negmr_b = [sbp(f"negmr_b{i}", [128, 512], BF16) for i in range(2)]
    b_negmrb = [p.buf() for _ in range(2)]
    t1 = sbp("t1", [128, 512], BF16); b_t1 = p.buf()
    wg_s = sbp("wg_s", [128, DcK * 8], BF16); b_wg = p.buf()
    fbias_s = sbp("fbias_s", [128, 32], F32); b_fbias = p.buf()
    iota_s = sbp("iota_s", [128, 32], F32); b_iota = p.buf()
    iotam_s = sbp("iotam_s", [128, 32], F32); b_iotam = p.buf()
    rt = sbp("rt", [128, 32], F32); b_rt = p.buf()
    rw = sbp("rw", [128, 32], F32); b_rw = p.buf()
    sc_ = sbp("sc_", [128, 32], F32); b_sc = p.buf()
    ta = sbp("ta", [128, 32], F32); b_ta = p.buf()
    tb = sbp("tb", [128, 32], F32); b_tb = p.buf()
    eqa = sbp("eqa", [128, 32], F32); b_eqa = p.buf()
    eqb = sbp("eqb", [128, 32], F32); b_eqb = p.buf()
    m1c = sbp("m1c", [128, 4], F32); b_m1 = p.buf()
    i1c = sbp("i1c", [128, 4], F32); b_i1 = p.buf()
    m2c = sbp("m2c", [128, 4], F32); b_m2 = p.buf()
    i2c = sbp("i2c", [128, 4], F32); b_i2 = p.buf()
    w1c = sbp("w1c", [128, 4], F32); b_w1 = p.buf()
    w2c = sbp("w2c", [128, 4], F32); b_w2 = p.buf()
    ewc = sbp("ewc", [128, 32], F32); b_ew = p.buf()
    ewbig = sbp("ewbig", [128, 128], F32); b_ewbig = p.buf()
    ps_s1 = [psp(f"ps_s1{i}", [128, 512], F32) for i in range(2)]
    b_pss1 = [p.buf() for _ in range(2)]
    ps_s2 = [psp(f"ps_s2{i}", [128, 512], F32) for i in range(2)]
    b_pss2 = [p.buf() for _ in range(2)]
    ps_rt = psp("ps_rt", [128, 512], F32); b_psrt = p.buf()
    ps_ew = psp("ps_ew", [128, 512], F32); b_psew = p.buf()
    b_rowsd = [p.buf() for _ in range(NT)]

    p.dma("gdma", wg_s.rearrange("q (c n) -> q c n", n=8),
          wg_t.rearrange("(c q) n -> q c n", q=128), writes=[b_wg])
    p.dma("sync", fbias_s[:, :].rearrange("q (t e) -> q t e", e=8),
          fbias_d.rearrange("(one e) -> one e", one=1).unsqueeze(1)
          .broadcast_to([128, 4, 8]), writes=[b_fbias])
    p.dma("sync", iota_s[:, :], iota8_d[:, :], writes=[b_iota])
    p.dma("sync", iotam_s[:, :], iota8m_d[:, :], writes=[b_iotam])
    p.emit("dve", lambda v: v.memset(ewbig[:, :], 0.0), writes=[b_ewbig])

    for tc in range(NT):
        sl = tc % 2
        tok = slice(tc * 512, (tc + 1) * 512)
        p.dma_split("sync", [(xc[sl][:, qu * 2048:(qu + 1) * 2048],
                              x_pk[tc, :, qu * 2048:(qu + 1) * 2048])
                             for qu in range(4)], b_xc[sl])
        for c in range(DcK):
            cs = slice(c * 512, (c + 1) * 512)
            p.emit("act", lambda s, cs=cs: s.copy(xb[sl][:, cs], xc[sl][:, cs]),
                   reads=[b_xc[sl]], writes=[b_xb[sl]])
            p.emit("dve", lambda v, cs=cs: v.tensor_mul(xsq[sl][:, cs], xb[sl][:, cs],
                                                        xb[sl][:, cs]),
                   reads=[b_xb[sl]], writes=[b_xsq[sl]])
        mm_group(p, ps_s1[sl][0:1, :],
                 [(ones_c[:, :], xb[sl][:, c * 512:(c + 1) * 512]) for c in range(DcK)],
                 reads=[b_ones, b_xb[sl]], writes=[b_pss1[sl]])
        mm_group(p, ps_s2[sl][0:1, :],
                 [(ones_c[:, :], xsq[sl][:, c * 512:(c + 1) * 512]) for c in range(DcK)],
                 reads=[b_ones, b_xsq[sl]], writes=[b_pss2[sl]])
        p.emit("dve", lambda v: v.tensor_scalar(srow[0:1, :], ps_s1[sl][0:1, :],
                                                1.0 / D, None, ALU.mult),
               reads=[b_pss1[sl]], writes=[b_srow])
        p.emit("dve", lambda v: v.tensor_scalar(sqrow[0:1, :], ps_s2[sl][0:1, :],
                                                1.0 / D, None, ALU.mult),
               reads=[b_pss2[sl]], writes=[b_sqrow])
        p.emit("dve", lambda v: v.tensor_mul(varrow[0:1, :], srow[0:1, :], srow[0:1, :]),
               reads=[b_srow], writes=[b_varrow])
        p.emit("dve", lambda v: v.tensor_sub(varrow[0:1, :], sqrow[0:1, :], varrow[0:1, :]),
               reads=[b_sqrow], writes=[b_varrow])
        p.emit("dve", lambda v: v.tensor_scalar(varrow[0:1, :], varrow[0:1, :], EPS,
                                                None, ALU.add),
               writes=[b_varrow])
        p.emit("act", lambda s: s.activation(varrow[0:1, :], varrow[0:1, :], AF.Ln),
               writes=[b_varrow])
        p.emit("act", lambda s: s.activation(rstd_r[0:1, :], varrow[0:1, :], AF.Exp,
                                             bias=0.0, scale=-0.5),
               reads=[b_varrow], writes=[b_rstd])
        p.emit("dve", lambda v: v.scalar_tensor_tensor(negmr_r[0:1, :], srow[0:1, :],
                                                       -1.0, rstd_r[0:1, :],
                                                       ALU.mult, ALU.mult),
               reads=[b_srow, b_rstd], writes=[b_negmr])
        p.dma("sync", rows_d[tc, 0:1, :], rstd_r[0:1, :], reads=[b_rstd],
              writes=[b_rowsd[tc]])
        p.dma("sync", rows_d[tc, 1:2, :], negmr_r[0:1, :], reads=[b_negmr],
              writes=[b_rowsd[tc]])
        p.dma("sync", rstd_b[sl][:, :], rows_d[tc, 0:1, :].broadcast_to([128, 512]),
              reads=[b_rowsd[tc]], writes=[b_rstdb[sl]])
        p.dma("sync", negmr_b[sl][:, :], rows_d[tc, 1:2, :].broadcast_to([128, 512]),
              reads=[b_rowsd[tc]], writes=[b_negmrb[sl]])
        for c in range(DcK):
            cs = slice(c * 512, (c + 1) * 512)
            p.emit("dve", lambda v, cs=cs: v.tensor_mul(t1[:, :], xb[sl][:, cs],
                                                        rstd_b[sl][:, :]),
                   reads=[b_xb[sl], b_rstdb[sl]], writes=[b_t1])
            p.emit("dve", lambda v, cs=cs: v.tensor_add(xnb[sl][:, cs], t1[:, :],
                                                        negmr_b[sl][:, :]),
                   reads=[b_t1, b_negmrb[sl]], writes=[b_xnb[sl]])
        for qu in range(2):
            csl = slice(qu * 4096, (qu + 1) * 4096)
            p.dma("sync", xn_pk[tc, :, csl], xnb[sl][:, csl], reads=[b_xnb[sl]])

        for tt in range(4):
            mm_group(p, ps_rt[:, tt * 8:(tt + 1) * 8],
                     [(xnb[sl][:, c * 512 + tt * 128: c * 512 + (tt + 1) * 128],
                       wg_s[:, c * 8:(c + 1) * 8]) for c in range(DcK)],
                     reads=[b_xnb[sl], b_wg], writes=[b_psrt])
        p.emit("dve", lambda v: v.tensor_copy(rt[:, :], ps_rt[:, 0:32]),
               reads=[b_psrt], writes=[b_rt])
        p.emit("act", lambda s: s.activation(rw[:, :], rt[:, :], AF.Exp,
                                             bias=0.0, scale=-1.0),
               reads=[b_rt], writes=[b_rw])
        p.emit("dve", lambda v: v.tensor_scalar(rw[:, :], rw[:, :], 1.0, None, ALU.add),
               writes=[b_rw])
        p.emit("dve", lambda v: v.reciprocal(rw[:, :], rw[:, :]), writes=[b_rw])
        p.emit("dve", lambda v: v.tensor_add(sc_[:, :], rw[:, :], fbias_s[:, :]),
               reads=[b_rw, b_fbias], writes=[b_sc])
        sg = lambda t_: t_[:, :].rearrange("q (t e) -> q t e", e=8)
        bc = lambda t_: t_[:, :].unsqueeze(2).broadcast_to([128, 4, 8])
        p.emit("dve", lambda v: v.reduce_max(m1c[:, :], sg(sc_), axis=AX.X),
               reads=[b_sc], writes=[b_m1])
        p.emit("dve", lambda v: v.tensor_tensor(ta[:, :], sc_[:, :], bc(m1c), ALU.is_ge),
               reads=[b_sc, b_m1], writes=[b_ta])
        p.emit("dve", lambda v: v.tensor_mul(tb[:, :], ta[:, :], iotam_s[:, :]),
               reads=[b_ta, b_iotam], writes=[b_tb])
        p.emit("dve", lambda v: v.tensor_scalar(tb[:, :], tb[:, :], 99.0, None, ALU.add),
               writes=[b_tb])
        p.emit("dve", lambda v: v.tensor_reduce(i1c[:, :], sg(tb), axis=AX.X, op=ALU.min),
               reads=[b_tb], writes=[b_i1])
        p.emit("dve", lambda v: v.tensor_tensor(eqa[:, :], iota_s[:, :], bc(i1c),
                                                ALU.is_equal),
               reads=[b_iota, b_i1], writes=[b_eqa])
        p.emit("dve", lambda v: v.scalar_tensor_tensor(ta[:, :], eqa[:, :], NEG,
                                                       sc_[:, :], ALU.mult, ALU.add),
               reads=[b_eqa, b_sc], writes=[b_ta])
        p.emit("dve", lambda v: v.reduce_max(m2c[:, :], sg(ta), axis=AX.X),
               reads=[b_ta], writes=[b_m2])
        p.emit("dve", lambda v: v.tensor_tensor(tb[:, :], ta[:, :], bc(m2c), ALU.is_ge),
               reads=[b_ta, b_m2], writes=[b_tb])
        p.emit("dve", lambda v: v.tensor_mul(tb[:, :], tb[:, :], iotam_s[:, :]),
               reads=[b_iotam], writes=[b_tb])
        p.emit("dve", lambda v: v.tensor_scalar(tb[:, :], tb[:, :], 99.0, None, ALU.add),
               writes=[b_tb])
        p.emit("dve", lambda v: v.tensor_reduce(i2c[:, :], sg(tb), axis=AX.X, op=ALU.min),
               reads=[b_tb], writes=[b_i2])
        p.emit("dve", lambda v: v.tensor_tensor(eqb[:, :], iota_s[:, :], bc(i2c),
                                                ALU.is_equal),
               reads=[b_iota, b_i2], writes=[b_eqb])
        p.emit("dve", lambda v: v.tensor_mul(ta[:, :], rw[:, :], eqa[:, :]),
               reads=[b_rw, b_eqa], writes=[b_ta])
        p.emit("dve", lambda v: v.reduce_sum(w1c[:, :], sg(ta), axis=AX.X),
               reads=[b_ta], writes=[b_w1])
        p.emit("dve", lambda v: v.tensor_mul(tb[:, :], rw[:, :], eqb[:, :]),
               reads=[b_rw, b_eqb], writes=[b_tb])
        p.emit("dve", lambda v: v.reduce_sum(w2c[:, :], sg(tb), axis=AX.X),
               reads=[b_tb], writes=[b_w2])
        p.emit("dve", lambda v: v.tensor_add(m1c[:, :], w1c[:, :], w2c[:, :]),
               reads=[b_w1, b_w2], writes=[b_m1])
        p.emit("dve", lambda v: v.reciprocal(m1c[:, :], m1c[:, :]), writes=[b_m1])
        p.emit("dve", lambda v: v.tensor_tensor(ta[:, :], eqa[:, :], bc(w1c), ALU.mult),
               reads=[b_eqa, b_w1], writes=[b_ta])
        p.emit("dve", lambda v: v.tensor_tensor(ta[:, :], ta[:, :], bc(m1c), ALU.mult),
               reads=[b_m1], writes=[b_ta])
        p.emit("dve", lambda v: v.tensor_tensor(tb[:, :], eqb[:, :], bc(w2c), ALU.mult),
               reads=[b_eqb, b_w2], writes=[b_tb])
        p.emit("dve", lambda v: v.tensor_tensor(tb[:, :], tb[:, :], bc(m1c), ALU.mult),
               reads=[b_m1], writes=[b_tb])
        p.emit("dve", lambda v: v.tensor_add(ewc[:, :], ta[:, :], tb[:, :]),
               reads=[b_ta, b_tb], writes=[b_ew])
        for tt in range(4):
            p.emit("dve", lambda v, tt=tt: v.tensor_copy(ewbig[:, 0:8],
                                                         ewc[:, tt * 8:(tt + 1) * 8]),
                   reads=[b_ew], writes=[b_ewbig])
            p.emit("pe", lambda t: t.transpose(ps_ew[:, 0:128], ewbig[:, :], id_f[:, :]),
                   reads=[b_ewbig, b_idf], writes=[b_psew])
            p.emit("dve", lambda v, tt=tt: v.tensor_copy(
                ewT_s[0:8, tc * 512 + tt * 128: tc * 512 + (tt + 1) * 128],
                ps_ew[0:8, 0:128]),
                   reads=[b_psew], writes=[b_ewT])
        pump_bg(1)

    p.dma("sync", ew_d[:, :], ewT_s[0:8, :], reads=[b_ewT])
    ph.close()
    p.barrier()

    ph = ExitStack()
    sbp = lambda name, shape, dt: ph.enter_context(nc.sbuf_tensor(name, shape, dt))
    psp = lambda name, shape, dt: ph.enter_context(nc.psum_tensor(name, shape, dt))
    ropec_s = sbp("ropec_s", [128, S], BF16); b_ropec = p.buf()
    ropes_s = sbp("ropes_s", [128, S], BF16); b_ropes = p.buf()
    xnc = [sbp(f"xnc{i}", [128, DcK * 512], BF16) for i in range(2)]
    b_xnc = [p.buf() for _ in range(2)]

    def kr_xn_load(tc):
        sl = tc % 2
        p.dma_split("sync", [(xnc[sl][:, qu * 4096:(qu + 1) * 4096],
                              xn_pk[tc, :, qu * 4096:(qu + 1) * 4096])
                             for qu in range(2)], b_xnc[sl])

    kr_xn_load(0)
    p.dma("sync", ropec_s[:, :], ropec_d[:, :], writes=[b_ropec])
    p.dma("sync", ropes_s[:, :], ropes_d[:, :], writes=[b_ropes])
    wkr_s = sbp("wkr_s", [128, DcK * D], BF16)
    b_wkrm = [p.buf() for _ in range(DcK)]
    nc.sync.wait_ge(bg_sem, 16 * 4)
    stream_weight_bf(wkrb_d, wkr_s, 4, D, DcK, b_wkrm)
    krp = [sbp(f"krp{i}", [128, 512], BF16) for i in range(4)]
    b_krp = [p.buf() for _ in range(4)]
    krsw = [sbp(f"krsw{i}", [128, 512], BF16) for i in range(2)]
    b_krsw = [p.buf() for _ in range(2)]
    rta = [sbp(f"rta{i}", [128, 512], BF16) for i in range(2)]
    b_rta = [p.buf() for _ in range(2)]
    krt = [sbp(f"krt{i}", [128, 512], BF16) for i in range(2)]
    b_krt = [p.buf() for _ in range(2)]
    ps_pr = [psp(f"ps_pr{i}", [128, 512], F32) for i in range(4)]
    b_pspr = [p.buf() for _ in range(4)]
    SWAP_MASK = [i ^ 1 for i in range(32)]

    def rope_chain(src_tile, b_src, s2, tok, out_dram_ap):
        p.emit("dve", lambda v: v.stream_shuffle(krsw[s2][:, :], src_tile, SWAP_MASK),
               reads=[b_src], writes=[b_krsw[s2]])
        p.emit("dve", lambda v: v.tensor_tensor(
            rta[s2][:, :], src_tile, ropec_s[:, tok], ALU.mult),
               reads=[b_src, b_ropec], writes=[b_rta[s2]])
        p.emit("dve", lambda v: v.tensor_tensor(
            krsw[s2][:, :], krsw[s2][:, :], ropes_s[:, tok], ALU.mult),
               reads=[b_ropes], writes=[b_krsw[s2]])
        p.emit("dve", lambda v: v.tensor_add(
            krt[s2][:, :], rta[s2][:, :], krsw[s2][:, :]),
               reads=[b_rta[s2], b_krsw[s2]], writes=[b_krt[s2]])
        p.dma("sync", out_dram_ap, krt[s2][:, :], reads=[b_krt[s2]])

    for tc in range(NT):
        sl = tc % 2
        tok = slice(tc * 512, (tc + 1) * 512)
        if tc + 1 < NT:
            kr_xn_load(tc + 1)
        for m in range(DcK):
            s4 = m % 4
            s2 = m % 2
            mm_group(p, ps_pr[s4][:, :],
                     [(wkr_s[:, c * D + m * 128: c * D + (m + 1) * 128],
                       xnc[sl][:, c * 512:(c + 1) * 512]) for c in range(DcK)],
                     reads=[b_wkrm[m], b_xnc[sl]], writes=[b_pspr[s4]])
            p.emit("act", lambda s, m=m, s4=s4: s.activation(
                krp[s4][:, :], ps_pr[s4][:, :], AF.Identity,
                bias=bias_s[:, 12 + m:13 + m], scale=1.0),
                   reads=[b_pspr[s4], b_bias], writes=[b_krp[s4]])
            rope_chain(krp[s4][:, :], b_krp[s4], s2, tok,
                       kr_d[m * 128:(m + 1) * 128, tok])
        pump_bg(2)
    ph.close()
    p.barrier()

    ph = ExitStack()
    sbp = lambda name, shape, dt: ph.enter_context(nc.sbuf_tensor(name, shape, dt))
    psp = lambda name, shape, dt: ph.enter_context(nc.psum_tensor(name, shape, dt))
    ropec_s = sbp("ropec_s2", [128, S], BF16); b_ropec = p.buf()
    ropes_s = sbp("ropes_s2", [128, S], BF16); b_ropes = p.buf()
    xnc2 = [sbp(f"xnc2_{i}", [128, DcK * 512], BF16) for i in range(2)]
    b_xnc2 = [p.buf() for _ in range(2)]
    p.dma_split("sync", [(xnc2[0][:, qu * 4096:(qu + 1) * 4096],
                          xn_pk[0, :, qu * 4096:(qu + 1) * 4096])
                         for qu in range(2)], b_xnc2[0])
    p.dma("sync", ropec_s[:, :], ropec_d[:, :], writes=[b_ropec])
    p.dma("sync", ropes_s[:, :], ropes_d[:, :], writes=[b_ropes])
    wdkv_s = sbp("wdkv_s", [128, DcK * DC], BF16)
    wdq_s = sbp("wdq_s", [128, DcK * DQC], BF16)
    wuk_s = sbp("wuk_s", [128, 4 * D], BF16)
    wuv_s = sbp("wuv_s", [128, 4 * D], BF16)
    wuq_s = sbp("wuq_s", [128, 8 * D], BF16)
    b_wdkvm = [p.buf() for _ in range(4)]
    b_wukm = [p.buf() for _ in range(DcK)]
    b_wuvm = [p.buf() for _ in range(4)]
    b_wdqm = [p.buf() for _ in range(8)]
    b_wuqm = [p.buf() for _ in range(DcK)]
    nc.sync.wait_ge(bg_sem, 16 * 11)
    stream_weight_bf(wdkvb_d, wdkv_s, 1, DC, DcK, b_wdkvm)
    stream_weight_bf(wukb_d, wuk_s, 4, D, 4, b_wukm)
    stream_weight_bf(wuvb_d, wuv_s, 1, D, 4, b_wuvm)
    stream_weight_bf(wdqb_d, wdq_s, 2, DQC, DcK, b_wdqm)
    stream_weight_bf(wuqb_d, wuq_s, 4, D, 8, b_wuqm)
    buvb = sbp("buvb", [128, 4 * 512], F32)
    b_buvb = p.buf()
    for n_ in range(4):
        p.dma("sync", buvb[:, n_ * 512:(n_ + 1) * 512],
              b_uv_d.rearrange("(one n) -> one n", one=1)[0:1, n_ * 512:(n_ + 1) * 512]
              .broadcast_to([128, 512]))
    p.seal("sync", b_buvb)
    ckv_c = sbp("ckv_c", [128, 4 * 512], BF16); b_ckv = p.buf()
    cq_c = sbp("cq_c", [128, 8 * 512], BF16); b_cq = p.buf()
    prj = [sbp(f"prj{i}", [128, 512], BF16) for i in range(4)]
    b_prj = [p.buf() for _ in range(4)]
    qsw = [sbp(f"qsw{i}", [128, 512], BF16) for i in range(2)]
    b_qsw = [p.buf() for _ in range(2)]
    rta2 = [sbp(f"rta2{i}", [128, 512], BF16) for i in range(2)]
    b_rta2 = [p.buf() for _ in range(2)]
    qrt = [sbp(f"qrt{i}", [128, 512], BF16) for i in range(2)]
    b_qrt = [p.buf() for _ in range(2)]
    vt = [sbp(f"vt{i}", [128, 512], BF16) for i in range(4)]
    b_vt = [p.buf() for _ in range(4)]
    ps_a = [psp(f"ps_a{i}", [128, 512], F32) for i in range(4)]
    b_psa = [p.buf() for _ in range(4)]
    ps_b = [psp(f"ps_b{i}", [128, 512], F32) for i in range(4)]
    b_psb = [p.buf() for _ in range(4)]
    SWAP_MASK2 = [i ^ 1 for i in range(32)]

    for tc in range(NT):
        sl = tc % 2
        tok = slice(tc * 512, (tc + 1) * 512)
        if tc + 1 < NT:
            p.dma_split("sync", [(xnc2[1 - sl][:, qu * 4096:(qu + 1) * 4096],
                                  xn_pk[tc + 1, :, qu * 4096:(qu + 1) * 4096])
                                 for qu in range(2)], b_xnc2[1 - sl])
        for m in range(4):
            s4 = m % 4
            mm_group(p, ps_a[s4][:, :],
                     [(wdkv_s[:, c * DC + m * 128: c * DC + (m + 1) * 128],
                       xnc2[sl][:, c * 512:(c + 1) * 512]) for c in range(DcK)],
                     reads=[b_wdkvm[m], b_xnc2[sl]], writes=[b_psa[s4]])
            p.emit("act", lambda s, m=m, s4=s4: s.activation(
                ckv_c[:, m * 512:(m + 1) * 512], ps_a[s4][:, :], AF.Identity,
                bias=bias_s[:, m:m + 1], scale=1.0),
                   reads=[b_psa[s4], b_bias], writes=[b_ckv])
        for m in range(DcK):
            s4 = m % 4
            mm_group(p, ps_a[s4][:, :],
                     [(wuk_s[:, c * D + m * 128: c * D + (m + 1) * 128],
                       ckv_c[:, c * 512:(c + 1) * 512]) for c in range(4)],
                     reads=[b_wukm[m], b_ckv], writes=[b_psa[s4]])
            p.emit("act", lambda s, m=m, s4=s4: s.activation(
                prj[s4][:, :], ps_a[s4][:, :], AF.Identity,
                bias=bias_s[:, 28 + m:29 + m], scale=1.0),
                   reads=[b_psa[s4], b_bias], writes=[b_prj[s4]])
            p.dma("sync", kc_d[m * 128:(m + 1) * 128, tok], prj[s4][:, :],
                  reads=[b_prj[s4]])
        for mt in range(4):
            for n_ in range(4):
                s4 = (mt * 4 + n_) % 4
                mm_group(p, ps_b[s4][:, :],
                         [(ckv_c[:, c * 512 + mt * 128: c * 512 + (mt + 1) * 128],
                           wuv_s[:, c * D + n_ * 512: c * D + (n_ + 1) * 512])
                          for c in range(4)],
                         reads=[b_ckv, b_wuvm[n_]], writes=[b_psb[s4]])
                p.emit("dve", lambda v, s4=s4, n_=n_: v.tensor_add(
                    vt[s4][:, :], ps_b[s4][:, :], buvb[:, n_ * 512:(n_ + 1) * 512]),
                       reads=[b_psb[s4], b_buvb], writes=[b_vt[s4]])
                p.dma("sync",
                      v_d[tc * 512 + mt * 128: tc * 512 + (mt + 1) * 128,
                          n_ * 512:(n_ + 1) * 512],
                      vt[s4][:, :], reads=[b_vt[s4]])
        for m in range(8):
            s4 = m % 4
            mm_group(p, ps_a[s4][:, :],
                     [(wdq_s[:, c * DQC + m * 128: c * DQC + (m + 1) * 128],
                       xnc2[sl][:, c * 512:(c + 1) * 512]) for c in range(DcK)],
                     reads=[b_wdqm[m], b_xnc2[sl]], writes=[b_psa[s4]])
            p.emit("act", lambda s, m=m, s4=s4: s.activation(
                cq_c[:, m * 512:(m + 1) * 512], ps_a[s4][:, :], AF.Identity,
                bias=bias_s[:, 4 + m:5 + m], scale=1.0),
                   reads=[b_psa[s4], b_bias], writes=[b_cq])
        for m in range(DcK):
            s4 = m % 4
            s2 = m % 2
            mm_group(p, ps_a[s4][:, :],
                     [(wuq_s[:, c * D + m * 128: c * D + (m + 1) * 128],
                       cq_c[:, c * 512:(c + 1) * 512]) for c in range(8)],
                     reads=[b_wuqm[m], b_cq], writes=[b_psa[s4]])
            p.emit("act", lambda s, m=m, s4=s4: s.activation(
                prj[s4][:, :], ps_a[s4][:, :], AF.Identity,
                bias=bias_s[:, 44 + m:45 + m], scale=1.0),
                   reads=[b_psa[s4], b_bias], writes=[b_prj[s4]])
            p.dma("sync", qc_d[m * 128:(m + 1) * 128, tok], prj[s4][:, :],
                  reads=[b_prj[s4]])
            p.emit("dve", lambda v, s4=s4, s2=s2: v.stream_shuffle(
                qsw[s2][:, :], prj[s4][:, :], SWAP_MASK2),
                   reads=[b_prj[s4]], writes=[b_qsw[s2]])
            p.emit("dve", lambda v, s4=s4, s2=s2: v.tensor_tensor(
                rta2[s2][:, :], prj[s4][:, :], ropec_s[:, tok], ALU.mult),
                   reads=[b_prj[s4], b_ropec], writes=[b_rta2[s2]])
            p.emit("dve", lambda v, s2=s2: v.tensor_tensor(
                qsw[s2][:, :], qsw[s2][:, :], ropes_s[:, tok], ALU.mult),
                   reads=[b_ropes], writes=[b_qsw[s2]])
            p.emit("dve", lambda v, s2=s2: v.tensor_add(
                qrt[s2][:, :], rta2[s2][:, :], qsw[s2][:, :]),
                   reads=[b_rta2[s2], b_qsw[s2]], writes=[b_qrt[s2]])
            p.dma("sync", qr_d[m * 128:(m + 1) * 128, tok], qrt[s2][:, :],
                  reads=[b_qrt[s2]])
        pump_bg(2)
    ph.close()
    p.barrier()

    ph = ExitStack()
    sbp = lambda name, shape, dt: ph.enter_context(nc.sbuf_tensor(name, shape, dt))
    psp = lambda name, shape, dt: ph.enter_context(nc.psum_tensor(name, shape, dt))
    masks_s = sbp("masks_s", [128, 4 * 512], F32); b_masks = p.buf()
    p.dma("sync", masks_s[:, :], masks_d[:, :], writes=[b_masks])
    kch = [sbp(f"kch{i}", [128, S], BF16) for i in range(2)]
    b_kch = [p.buf() for _ in range(2)]
    krh = [sbp(f"krh{i}", [128, S], BF16) for i in range(2)]
    b_krh = [p.buf() for _ in range(2)]
    qch = [sbp(f"qch{i}", [128, S], BF16) for i in range(2)]
    b_qch = [p.buf() for _ in range(2)]
    qrh = [sbp(f"qrh{i}", [128, S], BF16) for i in range(2)]
    b_qrh = [p.buf() for _ in range(2)]
    vh = [sbp(f"vh{i}", [128, S], BF16) for i in range(2)]
    b_vh = [p.buf() for _ in range(2)]
    ohp = [sbp(f"ohp{i}", [128, 512], BF16) for i in range(2)]
    b_ohp = [p.buf() for _ in range(2)]
    ptb = [sbp(f"ptb{i}", [128, (S // 128) * 512], BF16) for i in range(2)]
    b_ptb = [[p.buf() for _ in range(S // 128)] for _ in range(2)]
    srow = [sbp(f"srowa{i}", [1, 512], F32) for i in range(2)]
    b_srowa = [p.buf() for _ in range(2)]
    sumb = [sbp(f"sumb{i}", [128, 512], F32) for i in range(2)]
    b_sumb = [p.buf() for _ in range(2)]
    inv_d = nc.dram_tensor("inv_d", [2, 512], F32)
    b_invd = [p.buf() for _ in range(2)]
    ps_t = [psp(f"ps_t{i}", [128, 512], F32) for i in range(4)]
    b_pst = [p.buf() for _ in range(4)]
    ps_sum = [psp(f"ps_sum{i}", [128, 512], F32) for i in range(2)]
    b_pssum = [p.buf() for _ in range(2)]
    ps_o = [psp(f"ps_o{i}", [128, 512], F32) for i in range(2)]
    b_pso = [p.buf() for _ in range(2)]

    def attn_loads_kv(h, sl):
        hs = slice(h * 128, (h + 1) * 128)
        p.dma("sync", kch[sl][:, :], kc_d[hs, :], writes=[b_kch[sl]])
        p.dma("sync", krh[sl][:, :], kr_d[hs, :], writes=[b_krh[sl]])
        p.dma("sync", qch[sl][:, :], qc_d[hs, :], writes=[b_qch[sl]])
        p.dma("sync", qrh[sl][:, :], qr_d[hs, :], writes=[b_qrh[sl]])
        p.dma_split("sync", [
            (vh[sl].rearrange("q (t n) -> q t n", n=128)[:, tq * 8:(tq + 1) * 8, :],
             v_d.rearrange("(t q) d -> q t d", q=128)[:, tq * 8:(tq + 1) * 8, hs])
            for tq in range(4)], b_vh[sl])

    rr = {"t": 0}
    pend = []

    def flush():
        while pend:
            pend.pop(0)()

    def make_sum_pv(h, qs, sl):
        ntk = (qs + 1) * 4
        sp = qs % 2

        def sum_pv():
            def smbuild(t):
                last = None
                for tkc in range(ntk):
                    last = t.matmul(ps_sum[sp][0:1, :], ones_c[:, :],
                                    ptb[sp][:, tkc * 512:(tkc + 1) * 512],
                                    start=(tkc == 0), stop=(tkc == ntk - 1))
                return last
            p.emit("pe", smbuild,
                   reads=[b_ones] + [b_ptb[sp][tkc] for tkc in range(ntk)],
                   writes=[b_pssum[sp]])
            p.emit("dve", lambda v: v.tensor_copy(srow[sp][0:1, :],
                                                  ps_sum[sp][0:1, :]),
                   reads=[b_pssum[sp]], writes=[b_srowa[sp]])
            p.emit("dve", lambda v: v.reciprocal(srow[sp][0:1, :], srow[sp][0:1, :]),
                   writes=[b_srowa[sp]])
            p.dma("sync", inv_d[sp:sp + 1, :], srow[sp][0:1, :],
                  reads=[b_srowa[sp]], writes=[b_invd[sp]])
            p.dma("sync", sumb[sp][:, :],
                  inv_d[sp:sp + 1, :].broadcast_to([128, 512]),
                  reads=[b_invd[sp]], writes=[b_sumb[sp]])
            mm_group(p, ps_o[sp][:, :],
                     [(vh[sl][:, tkc * 128:(tkc + 1) * 128],
                       ptb[sp][:, tkc * 512:(tkc + 1) * 512]) for tkc in range(ntk)],
                     reads=[b_vh[sl]] + [b_ptb[sp][tkc] for tkc in range(ntk)],
                     writes=[b_pso[sp]])
        return sum_pv

    def make_odiv(h, qs, sl):
        sp = qs % 2

        def odiv():
            p.emit("dve", lambda v: v.tensor_mul(ohp[sp][:, :], ps_o[sp][:, :],
                                                 sumb[sp][:, :]),
                   reads=[b_pso[sp], b_sumb[sp]], writes=[b_ohp[sp]])
            p.dma("sync", o_d[h * 128:(h + 1) * 128, qs * 512:(qs + 1) * 512],
                  ohp[sp][:, :], reads=[b_ohp[sp]])
        return odiv

    attn_loads_kv(0, 0)
    for h in range(H):
        sl = h % 2
        for qs in range(NQS):
            if qs == 1 and h + 1 < H:
                attn_loads_kv(h + 1, (h + 1) % 2)
                pump_bg(3)
            ntk = (qs + 1) * 4
            sp = qs % 2
            qsl5 = slice(qs * 512, (qs + 1) * 512)
            for tkc in range(ntk):
                bank = rr["t"] % 4
                rr["t"] += 1
                tsl = slice(tkc * 128, (tkc + 1) * 128)
                mm_group(p, ps_t[bank][:, :],
                         [(kch[sl][:, tsl], qch[sl][:, qsl5]),
                          (krh[sl][:, tsl], qrh[sl][:, qsl5])],
                         reads=[b_kch[sl], b_krh[sl], b_qch[sl], b_qrh[sl]],
                         writes=[b_pst[bank]])
                i = tkc - qs * 4
                if i >= 0:
                    p.emit("dve", lambda v, bank=bank, i=i: v.tensor_add(
                        ps_t[bank][:, :], ps_t[bank][:, :],
                        masks_s[:, i * 512:(i + 1) * 512]),
                           reads=[b_masks], writes=[b_pst[bank]])
                p.emit("act", lambda s, bank=bank, tkc=tkc, sp=sp: s.activation(
                    ptb[sp][:, tkc * 512:(tkc + 1) * 512], ps_t[bank][:, :], AF.Exp,
                    bias=0.0, scale=1.0 / 16.0),
                       reads=[b_pst[bank]], writes=[b_ptb[sp][tkc]])
                if tkc == 3 and pend:
                    pend.pop(0)()
                if tkc == 7 and pend:
                    pend.pop(0)()
            pend.append(make_sum_pv(h, qs, sl))
            pend.append(make_odiv(h, qs, sl))
            while len(pend) > 2:
                pend.pop(0)()
    flush()
    ph.close()
    p.barrier()

    ph = ExitStack()
    sbp = lambda name, shape, dt: ph.enter_context(nc.sbuf_tensor(name, shape, dt))
    psp = lambda name, shape, dt: ph.enter_context(nc.psum_tensor(name, shape, dt))
    oc = [sbp(f"oc{i}", [128, DcK * 512], BF16) for i in range(2)]
    b_oc = [p.buf() for _ in range(2)]
    xc2 = [sbp(f"xc2{i}", [128, DcK * 512], F32) for i in range(2)]
    b_xc2 = [p.buf() for _ in range(2)]
    t2 = [sbp(f"t2{i}", [128, 512], F32) for i in range(4)]
    b_t2 = [p.buf() for _ in range(4)]
    ps_w = [psp(f"ps_w{i}", [128, 512], F32) for i in range(4)]
    b_psw2 = [p.buf() for _ in range(4)]

    def wo_loads(tc):
        sl = tc % 2
        tok = slice(tc * 512, (tc + 1) * 512)
        p.dma("sync", oc[sl].rearrange("q (c n) -> q c n", n=512),
              o_d.rearrange("(c q) s -> q c s", q=128)[:, :, tok], writes=[b_oc[sl]])
        p.dma_split("sync", [(xc2[sl][:, qu * 2048:(qu + 1) * 2048],
                              x_pk[tc, :, qu * 2048:(qu + 1) * 2048])
                             for qu in range(4)], b_xc2[sl])

    wo_loads(0)
    wo_s = sbp("wo_s", [128, DcK * D], BF16)
    b_wom = [p.buf() for _ in range(DcK)]
    nc.sync.wait_ge(bg_sem, 16 * N_BG_ATTN)
    stream_weight_bf(wob_d, wo_s, 4, D, DcK, b_wom)
    for tc in range(NT):
        sl = tc % 2
        tok = slice(tc * 512, (tc + 1) * 512)
        for m in range(DcK):
            if m == 8 and tc + 1 < NT:
                wo_loads(tc + 1)
            s4 = m % 4
            mm_group(p, ps_w[s4][:, :],
                     [(wo_s[:, c * D + m * 128: c * D + (m + 1) * 128],
                       oc[sl][:, c * 512:(c + 1) * 512]) for c in range(DcK)],
                     reads=[b_wom[m], b_oc[sl]], writes=[b_psw2[s4]])
            p.emit("act", lambda s, m=m, s4=s4: s.activation(
                t2[s4][:, :], ps_w[s4][:, :], AF.Identity,
                bias=bias_s[:, 60 + m:61 + m], scale=1.0),
                   reads=[b_psw2[s4], b_bias], writes=[b_t2[s4]])
            p.emit("dve", lambda v, m=m, s4=s4: v.tensor_add(
                t2[s4][:, :], t2[s4][:, :],
                xc2[sl][:, m * 512:(m + 1) * 512]),
                   reads=[b_xc2[sl]], writes=[b_t2[s4]])
            p.dma("sync", part_d[m * 128:(m + 1) * 128, tok], t2[s4][:, :],
                  reads=[b_t2[s4]])
    ph.close()
    p.barrier()

    ph = ExitStack()
    sbp = lambda name, shape, dt: ph.enter_context(nc.sbuf_tensor(name, shape, dt))
    psp = lambda name, shape, dt: ph.enter_context(nc.psum_tensor(name, shape, dt))
    xnf = sbp("xnf", [128, DcK * CF], BF16); b_xnf = p.buf()
    acc = sbp("acc", [128, DcK * CF], F32)
    b_accm = [p.buf() for _ in range(DcK)]
    g_s = sbp("g_s", [128, 8 * CF], BF16)
    b_gs = [p.buf() for _ in range(8)]
    w13 = [sbp(f"w13{i}", [128, DcK * 256], BF16) for i in range(4)]
    b_w13 = [p.buf() for _ in range(4)]
    w2s = [sbp(f"w2s{i}", [128, 8 * 256], BF16) for i in range(2)]
    b_w2s = [p.buf() for _ in range(2)]
    ewb = [sbp(f"ewb{i}", [128, CF], BF16) for i in range(2)]
    b_ewb = [p.buf() for _ in range(2)]
    silt = [sbp(f"silt{i}", [128, 512], BF16) for i in range(2)]
    b_sil = [p.buf() for _ in range(2)]
    h3t = [sbp(f"h3t{i}", [128, 512], BF16) for i in range(2)]
    b_h3 = [p.buf() for _ in range(2)]
    gtmp = [sbp(f"gtmp{i}", [128, 512], BF16) for i in range(2)]
    b_gt = [p.buf() for _ in range(2)]
    ps_1 = [psp(f"ps_1{i}", [128, 512], F32) for i in range(2)]
    b_ps1 = [p.buf() for _ in range(2)]
    ps_3 = [psp(f"ps_3{i}", [128, 512], F32) for i in range(2)]
    b_ps3 = [p.buf() for _ in range(2)]
    ps_2 = [psp(f"ps_2{i}", [128, 512], F32) for i in range(2)]
    b_ps2 = [p.buf() for _ in range(2)]

    pump_bg(len(bg_jobs))
    nc.sync.wait_ge(bg_sem, 16 * len(bg_jobs))

    w1b_r = w1b_d.rearrange("e (c q) n -> e q c n", q=128)
    w3b_r = w3b_d.rearrange("e (c q) n -> e q c n", q=128)
    w2b_r = w2b_d.rearrange("e (c q) n -> e q c n", q=128)

    part_r = part_d.rearrange("(c q) s -> q c s", q=128)
    out_r = out_fm.rearrange("(c q) s -> q c s", q=128)
    for cf in range(NFC):
        tok = slice(cf * CF, (cf + 1) * CF)
        if cf == 0:
            p.dma_split("sync", [(xnf[:, nt * 8192:(nt + 1) * 8192],
                                  xn_pk[cf * NTC + nt, :, :])
                                 for nt in range(NTC)], b_xnf)
        for e_ in range(E):
            se_ = e_ % 2
            p.dma("sync", ewb[se_][:, :],
                  ew_d[e_:e_ + 1, tok].broadcast_to([128, CF]), writes=[b_ewb[se_]])
            for fc2 in range(4):
                w1sl = w13[(fc2 % 2) * 2]
                w3sl = w13[(fc2 % 2) * 2 + 1]
                bw1 = b_w13[(fc2 % 2) * 2]
                bw3 = b_w13[(fc2 % 2) * 2 + 1]
                fsl = slice(fc2 * 256, (fc2 + 1) * 256)
                if not (cf > 0 and e_ == 0 and fc2 == 0):
                    p.dma("sync", w1sl.rearrange("q (c n) -> q c n", n=256),
                          w1b_r[e_, :, :, fsl], writes=[bw1])
                    p.dma("sync", w3sl.rearrange("q (c n) -> q c n", n=256),
                          w3b_r[e_, :, :, fsl], writes=[bw3])
                if cf == 0 and e_ == 0 and fc2 == 1:
                    for m in range(DcK):
                        p.dma("sync" if m % 2 == 0 else "gdma",
                              acc[:, m * CF:(m + 1) * CF], part_r[:, m, tok],
                              writes=[b_accm[m]])
                for sm in range(2):
                    fm = fc2 * 2 + sm
                    for nt in range(NTC):
                        s2 = nt % 2
                        tsl = slice(nt * 512, (nt + 1) * 512)
                        mm_group(p, ps_1[s2][:, :],
                                 [(w1sl[:, c * 256 + sm * 128: c * 256 + (sm + 1) * 128],
                                   xnf[:, nt * 8192 + c * 512: nt * 8192 + (c + 1) * 512])
                                  for c in range(DcK)],
                                 reads=[bw1, b_xnf], writes=[b_ps1[s2]])
                        mm_group(p, ps_3[s2][:, :],
                                 [(w3sl[:, c * 256 + sm * 128: c * 256 + (sm + 1) * 128],
                                   xnf[:, nt * 8192 + c * 512: nt * 8192 + (c + 1) * 512])
                                  for c in range(DcK)],
                                 reads=[bw3, b_xnf], writes=[b_ps3[s2]])
                        p.emit("act", lambda s, s2=s2: s.activation(
                            silt[s2][:, :], ps_1[s2][:, :], AF.Silu),
                               reads=[b_ps1[s2]], writes=[b_sil[s2]])
                        p.emit("dve", lambda v, s2=s2, tsl=tsl: v.tensor_mul(
                            h3t[s2][:, :], ps_3[s2][:, :], ewb[se_][:, tsl]),
                               reads=[b_ps3[s2], b_ewb[se_]], writes=[b_h3[s2]])
                        p.emit("dve", lambda v, s2=s2, fm=fm, nt=nt: v.tensor_mul(
                            g_s[:, fm * CF + nt * 512: fm * CF + (nt + 1) * 512],
                            silt[s2][:, :], h3t[s2][:, :]),
                               reads=[b_sil[s2], b_h3[s2]], writes=[b_gs[fm]])
            for m2 in range(8):
                sw = m2 % 2
                msl = slice(m2 * 256, (m2 + 1) * 256)
                p.dma("sync", w2s[sw].rearrange("q (c n) -> q c n", n=256),
                      w2b_r[e_, :, :, msl], writes=[b_w2s[sw]])
                if e_ == E - 1 and m2 == 0 and cf + 1 < NFC:
                    p.dma_split("gdma", [(xnf[:, nt * 8192:(nt + 1) * 8192],
                                          xn_pk[(cf + 1) * NTC + nt, :, :])
                                         for nt in range(NTC)], b_xnf)
                if e_ == E - 1 and m2 == 4 and cf + 1 < NFC:
                    p.dma("sync", w13[0].rearrange("q (c n) -> q c n", n=256),
                          w1b_r[0, :, :, 0:256], writes=[b_w13[0]])
                    p.dma("sync", w13[1].rearrange("q (c n) -> q c n", n=256),
                          w3b_r[0, :, :, 0:256], writes=[b_w13[1]])
                for sm in range(2):
                    m = m2 * 2 + sm
                    for nt in range(NTC):
                        s2 = nt % 2
                        mm_group(p, ps_2[s2][:, :],
                                 [(w2s[sw][:, c * 256 + sm * 128: c * 256 + (sm + 1) * 128],
                                   g_s[:, c * CF + nt * 512: c * CF + (nt + 1) * 512])
                                  for c in range(8)],
                                 reads=[b_w2s[sw]] + b_gs, writes=[b_ps2[s2]])
                        p.emit("dve", lambda v, s2=s2, m=m, nt=nt: v.tensor_add(
                            acc[:, m * CF + nt * 512: m * CF + (nt + 1) * 512],
                            acc[:, m * CF + nt * 512: m * CF + (nt + 1) * 512],
                            ps_2[s2][:, :]),
                               reads=[b_ps2[s2]], writes=[b_accm[m]])
                    if e_ == E - 1:
                        p.dma("sync" if m % 2 == 0 else "gdma",
                              out_r[:, m, tok], acc[:, m * CF:(m + 1) * CF],
                              reads=[b_accm[m]])
                        if cf + 1 < NFC:
                            ntok = slice((cf + 1) * CF, (cf + 2) * CF)
                            p.dma("sync" if m % 2 == 1 else "gdma",
                                  acc[:, m * CF:(m + 1) * CF], part_r[:, m, ntok],
                                  writes=[b_accm[m]])
    for q, n in DMA_POOL.items():
        eng = nc.sync if q == "sync" else nc.gpsimd
        for j in range(n):
            key = f"{q}{j}"
            if p.cnt[key] > 0:
                eng.wait_ge(p.sems[key], p.cnt[key])
    ph.close()
    stack.close()
    nc.finalize()
    return nc



def _host_consts(S):
    bf = ml_dtypes.bfloat16
    ident = np.eye(128)
    pswap = np.zeros((128, 128), dtype=np.float32)
    for k in range(128):
        pswap[k, k ^ 1] = 1.0
    ones_col = np.ones((128, 1), dtype=np.float32)
    idx = np.arange(64, dtype=np.float32)
    freqs = np.float32(10000.0) ** (np.float32(-2.0) * idx / np.float32(128.0))
    pos = np.arange(S, dtype=np.float32)
    theta = pos[:, None] * freqs[None, :]
    cosv = np.cos(theta.astype(np.float64)).astype(np.float32)
    sinv = np.sin(theta.astype(np.float64)).astype(np.float32)
    ropec = np.repeat(cosv.T, 2, axis=0)
    base = np.repeat(sinv.T, 2, axis=0)
    sign = np.where(np.arange(128) % 2 == 0, -1.0, 1.0).astype(np.float32)
    ropes = base * sign[:, None]
    masks = np.zeros((128, 4 * 512), dtype=np.float32)
    j = np.arange(512)
    for i in range(4):
        prow = i * 128 + np.arange(128)
        masks[:, i * 512:(i + 1) * 512] = np.where(
            j[None, :] >= prow[:, None], 0.0, NEG).astype(np.float32)
    iota = np.tile(np.arange(8, dtype=np.float32), (128, 4))
    return {
        "ident_bf": ident.astype(bf), "ident_f32": ident.astype(np.float32),
        "pswap": pswap.astype(bf), "ones_col": ones_col.astype(bf),
        "ropec": ropec.astype(bf), "ropes": ropes.astype(bf),
        "masks": masks, "iota8": iota,
        "iota8m": (iota - 99.0).astype(np.float32),
    }


def host_inputs(g, S):
    f32 = np.float32
    shared = {
        "wdkv_t": np.ascontiguousarray(g["L0_attn_W_DKV_w"].T.astype(f32)),
        "wuk_t": np.ascontiguousarray(g["L0_attn_W_UK_w"].T.astype(f32)),
        "wuv_t": np.ascontiguousarray(g["L0_attn_W_UV_w"].T.astype(f32)),
        "wdq_t": np.ascontiguousarray(g["L0_attn_W_DQ_w"].T.astype(f32)),
        "wuq_t": np.ascontiguousarray(g["L0_attn_W_UQ_w"].T.astype(f32)),
        "wkr_t": np.ascontiguousarray(g["L0_attn_W_KR_w"].T.astype(f32)),
        "wo_t": np.ascontiguousarray(g["L0_attn_W_O_w"].T.astype(f32)),
        "wg_t": np.ascontiguousarray(g["L0_ffn_Wg"].T.astype(f32)),
        "w1_t": np.ascontiguousarray(np.transpose(g["L0_ffn_W1"], (0, 2, 1)).astype(f32)),
        "w3_t": np.ascontiguousarray(np.transpose(g["L0_ffn_W3"], (0, 2, 1)).astype(f32)),
        "w2_t": np.ascontiguousarray(np.transpose(g["L0_ffn_W2"], (0, 2, 1)).astype(f32)),
        "b_uv": np.ascontiguousarray(g["L0_attn_W_UV_b"].astype(f32)),
        "fbias": np.ascontiguousarray(g["L0_ffn_bias"].astype(f32)),
    }
    bias_pk = np.zeros((128, 76), dtype=f32)
    for key, col0, n in (("L0_attn_W_DKV_b", 0, 4), ("L0_attn_W_DQ_b", 4, 8),
                         ("L0_attn_W_KR_b", 12, 16), ("L0_attn_W_UK_b", 28, 16),
                         ("L0_attn_W_UQ_b", 44, 16), ("L0_attn_W_O_b", 60, 16)):
        bias_pk[:, col0:col0 + n] = g[key].astype(f32).reshape(n, 128).T
    shared["bias_pk"] = np.ascontiguousarray(bias_pk)
    shared.update(_host_consts(S))
    return shared


def kernel(**inputs):
    g = {k: np.asarray(v) for k, v in inputs.items()}
    x = g["L0_attn_h"]
    B, S_, D_ = x.shape
    assert D_ == D and B == NCORE
    shared = host_inputs(g, S_)
    in_maps = []
    NT_ = S_ // 512
    for b in range(B):
        m = dict(shared)
        xb_ = x[b].astype(np.float32).reshape(NT_, 512, DcK, 128)
        m["x_pk"] = np.ascontiguousarray(
            xb_.transpose(0, 3, 2, 1).reshape(NT_, 128, DcK * 512))
        in_maps.append(m)
    nc = build_program(S=S_)
    res = run_bass_kernel_spmd(nc, in_maps, core_ids=list(range(NCORE)))
    out = np.empty((B, S_, D), dtype=np.float32)
    for b in range(B):
        out[b] = np.asarray(res.results[b]["out_fm"]).T
    return out
